# version 01
# baseline (speedup 1.0000x reference)
import numpy as np
import ml_dtypes
import concourse.bass as bass
import concourse.mybir as mybir
from concourse.bass_utils import run_bass_kernel_spmd
from contextlib import ExitStack

F32 = mybir.dt.float32
BF16 = mybir.dt.bfloat16
AF = mybir.ActivationFunctionType
ALU = mybir.AluOpType
bf16 = ml_dtypes.bfloat16

DBG = []

C = 1024
H = 16
NCC = 8
SC = 512
TB = 4
LP = 16
XR = 5
EPS = 1e-5


def build_program(nsc):
    T = nsc * SC
    NQ = nsc * TB
    nc = bass.Bass("TRN2", target_bir_lowering=False, debug=False,
                   detect_race_conditions=False)

    x_h = nc.dram_tensor("x", [T, C], F32, kind="ExternalInput").ap()
    w_h = {}
    for nm in ("wkT", "wvT", "wrT", "woT", "wfT"):
        w_h[nm] = nc.dram_tensor(nm, [C, C], F32, kind="ExternalInput").ap()
    mk_h = nc.dram_tensor("mkc", [128, NCC], F32, kind="ExternalInput").ap()
    mv_h = nc.dram_tensor("mvc", [128, NCC], F32, kind="ExternalInput").ap()
    mr_h = nc.dram_tensor("mrc", [128, NCC], F32, kind="ExternalInput").ap()
    uc_h = nc.dram_tensor("ucols", [128, NCC], F32, kind="ExternalInput").ap()
    td_h = nc.dram_tensor("tdcol", [H, 1], F32, kind="ExternalInput").ap()
    tf_h = nc.dram_tensor("tfcol", [H, 1], F32, kind="ExternalInput").ap()
    par_h = nc.dram_tensor("parity", [H, 128], F32, kind="ExternalInput").ap()
    hm8_h = nc.dram_tensor("hm8", [H, NCC], F32, kind="ExternalInput").ap()
    hm16_h = nc.dram_tensor("hm16", [NCC, 128, H], BF16, kind="ExternalInput").ap()
    sel2_h = nc.dram_tensor("sel2", [NCC, H, 128], BF16, kind="ExternalInput").ap()
    out_h = nc.dram_tensor("out", [T, C], F32, kind="ExternalOutput").ap()

    es = ExitStack()

    def sb(name, shape, dt):
        return es.enter_context(nc.sbuf_tensor(name, shape, dt))

    def ps(name, shape):
        return es.enter_context(nc.psum_tensor(name, shape, F32))

    def sem(name):
        return es.enter_context(nc.semaphore(name))

    with es:
        wk = sb("wk", [128, NCC, C], BF16)
        wv = sb("wv", [128, NCC, C], BF16)
        wr = sb("wr", [128, NCC, C], BF16)
        wo = sb("wo", [128, NCC, C], BF16)
        wf = sb("wf", [128, NCC, C], BF16)
        x_tm = sb("x_tm", [128, XR, C], F32)
        tm_stage = sb("tm_stage", [128, 2, C], BF16)
        ln1_fm = sb("ln1_fm", [128, NCC, SC + LP], BF16)
        diff = sb("diff", [128, NCC, SC], BF16)
        xmk = sb("xmk", [128, NCC, SC], BF16)
        xmv = sb("xmv", [128, NCC, SC], BF16)
        ekri = sb("ekri", [128, NCC, SC], BF16)
        vvnum = sb("vvnum", [128, NCC, SC], BF16)
        xres = sb("xres", [128, TB, C], BF16)
        a_ext = sb("a_ext", [128, NCC, SC + 1], BF16)
        big0 = sb("big0", [128, TB * C], F32)
        rwkv = sb("rwkv", [128, NCC, SC], BF16)
        out_r = sb("out_r", [128, C], F32)
        dump = sb("dump", [128, 512], BF16)
        lam_ch = sb("lam_ch", [128, NCC], F32)
        eu_ch = sb("eu_ch", [128, NCC], F32)
        mkc = sb("mkc_s", [128, NCC], F32)
        mvc = sb("mvc_s", [128, NCC], F32)
        mrc = sb("mrc_s", [128, NCC], F32)
        ucs = sb("ucs", [128, NCC], F32)
        carry = sb("carry", [128, NCC], F32)
        tdc = sb("tdc", [H, 1], F32)
        tfc = sb("tfc", [H, 1], F32)
        e16 = sb("e16", [H, 1], F32)
        lam16 = sb("lam16", [H, 1], F32)
        eu16 = sb("eu16", [H, 1], F32)
        lrhs = sb("lrhs", [H, NCC], F32)
        hm8 = sb("hm8_s", [H, NCC], F32)
        hm16 = sb("hm16_s", [128, NCC, H], BF16)
        sel2 = sb("sel2_s", [H, NCC, 128], BF16)
        b_ext_t = sb("b_ext", [H, SC + 1], F32)
        den_so_t = sb("den_so", [H, SC], F32)
        b_ext = b_ext_t[:]
        den_so = den_so_t[:]
        dcar = sb("dcar", [H, 1], F32)
        consts = sb("consts", [128, 2], F32)
        st1 = sb("st1", [128, 6, TB], F32)
        st2 = sb("st2", [128, 6, TB], F32)
        st3 = sb("st3", [128, 6, TB], F32)
        st4 = sb("st4", [128, 6, TB], F32)

        tens = dict(wk=wk, wv=wv, wr=wr, wo=wo, wf=wf, x_tm=x_tm,
                    tm_stage=tm_stage, ln1_fm=ln1_fm, diff=diff, xmk=xmk,
                    xmv=xmv, ekri=ekri, vvnum=vvnum, xres=xres, a_ext=a_ext,
                    big0=big0, rwkv=rwkv, out_r=out_r, dump=dump,
                    lam_ch=lam_ch, eu_ch=eu_ch, carry=carry, lam16=lam16,
                    eu16=eu16, b_ext=b_ext_t, den_so=den_so_t, dcar=dcar,
                    st1=st1, st2=st2, st3=st3, st4=st4, hm16=hm16, sel2=sel2)
        dbg_out = {}
        for nm in DBG:
            t = tens[nm]
            dbg_out[nm] = nc.dram_tensor(
                "dbg_" + nm, list(t[:].shape), t[:].dtype,
                kind="ExternalOutput").ap()

        pb_kvr = [ps("pb_kvr0", [128, 512]), ps("pb_kvr1", [128, 512])]
        pb_olo = [ps("pb_olo0", [128, 512]), ps("pb_olo1", [128, 512])]
        pb_ohi = [ps("pb_ohi0", [128, 512]), ps("pb_ohi1", [128, 512])]
        pb_b = ps("pb_b", [H, 512])
        pb_r = ps("pb_r", [128, 512])

        s_cst = sem("s_cst")
        s_wc = [sem(f"s_wc{i}") for i in range(4)]
        s_slot = [sem(f"s_slot{j}") for j in range(XR)]
        slot_exp = [0] * XR
        xq_ready = {}
        outd_done = {}
        wf_done = {}
        x1f_cnt = [0]
        x3f_cnt = [0]
        x1f_done = {}
        x3f_done = {}
        tm_reader = {}
        s_wfsg = sem("s_wfsg")
        s_cmem = sem("s_cmem")
        s_lam = sem("s_lam")
        s_lch = sem("s_lch")
        s_l1b = sem("s_l1b")
        s_l1ap = sem("s_l1ap")
        s_x1f = sem("s_x1f")
        s_diff = sem("s_diff")
        s_sq1 = sem("s_sq1")
        s_ch1 = sem("s_ch1")
        s_le1 = sem("s_le1")
        s_nb1 = sem("s_nb1")
        s_mixp = sem("s_mixp")
        s_mix = sem("s_mix")
        s_kvrmm = sem("s_kvrmm")
        s_kvrev = sem("s_kvrev")
        s_bmm = sem("s_bmm")
        s_bext = sem("s_bext")
        s_den = sem("s_den")
        s_rmm = sem("s_rmm")
        s_rinv = sem("s_rinv")
        s_rwkv = sem("s_rwkv")
        s_aTT = sem("s_aTT")
        s_vx = sem("s_vx")
        s_xcp = sem("s_xcp")
        s_omm = sem("s_omm")
        s_ostat = sem("s_ostat")
        s_ch2 = sem("s_ch2")
        s_le2 = sem("s_le2")
        s_nb2 = sem("s_nb2")
        s_xnb = sem("s_xnb")
        s_xp = sem("s_xp")
        s_sq3 = sem("s_sq3")
        s_ch3 = sem("s_ch3")
        s_le3 = sem("s_le3")
        s_nb3 = sem("s_nb3")
        s_l3ap = sem("s_l3ap")
        s_x3f = sem("s_x3f")
        s_fmm = sem("s_fmm")
        s_fstat = sem("s_fstat")
        s_ch4 = sem("s_ch4")
        s_le4 = sem("s_le4")
        s_nb4 = sem("s_nb4")
        s_xnb4 = sem("s_xnb4")
        s_out = sem("s_out")
        s_od = sem("s_od")

        nc.const_aps.register(F32, -0.5, consts[:, 0:1])
        nc.const_aps.register(F32, -1.0, consts[:, 1:2])

        block = es.enter_context(nc.Block())

        vx_n = [0]

        def vsync(vector, inst):
            vx_n[0] += 1
            inst.then_inc(s_vx, 1)
            vector.wait_ge(s_vx, vx_n[0])

        scan_v = big0[:].rearrange("p (a b) -> p a b", a=NCC)
        xp_v = big0[:].rearrange("p (a b) -> p a b", a=TB)

        def wseq(eng, semh, val):
            if val > 0:
                eng.wait_ge(semh, val)

        pair_usages = []

        def pair_wait(tensor, kind, s, tb):
            pair_usages.append((kind, s, tb))
            g = len(pair_usages) - 1
            if g >= 2:
                k2, s2, tb2 = pair_usages[g - 2]
                tau2 = s2 * TB + tb2
                if k2 == "o":
                    tensor.wait_ge(s_ostat, tau2 + 1)
                    tensor.wait_ge(s_xp, tau2 + 1)
                else:
                    tensor.wait_ge(s_fstat, tau2 + 1)
                    tensor.wait_ge(s_out, tau2 + 1)

        @block.sync
        def _(sync):
            for ap_h, t in ((mk_h, mkc), (mv_h, mvc), (mr_h, mrc),
                            (uc_h, ucs), (td_h, tdc), (tf_h, tfc),
                            (hm8_h, hm8)):
                sync.dma_start(out=t[:], in_=ap_h).then_inc(s_cst, 16)
            sync.dma_start(out=big0[0:16, 0:128], in_=par_h
                           ).then_inc(s_cst, 16)
            sync.dma_start(out=hm16[:], in_=hm16_h.transpose([1, 0, 2])
                           ).then_inc(s_cst, 16)
            sync.dma_start(out=sel2[:], in_=sel2_h.transpose([1, 0, 2])
                           ).then_inc(s_cst, 16)
            for ic in range(NCC):
                if ic >= 2:
                    sync.wait_ge(s_wfsg, ic - 1)
                j = ic % 2
                slot_exp[j] += 16
                wf_done[ic] = slot_exp[j]
                sync.dma_start(out=x_tm[:, j, :],
                               in_=w_h["wfT"][ic * 128:(ic + 1) * 128, :]
                               ).then_inc(s_slot[j], 16)
            sync.wait_ge(s_wfsg, NCC)
            for it in range(-1, nsc):
                sf = it + 1
                if sf < nsc:
                    for tb in range(TB):
                        q = sf * TB + tb
                        j = q % XR
                        wseq(sync, s_l1ap, q - XR + 1)
                        wseq(sync, s_xcp, q - XR + 1)
                        if q - XR >= 0:
                            sync.wait_ge(s_nb1, (q - XR) // TB + 1)
                        slot_exp[j] += 16
                        xq_ready[q] = slot_exp[j]
                        sync.dma_start(out=x_tm[:, j, :],
                                       in_=x_h[q * 128:(q + 1) * 128, :]
                                       ).then_inc(s_slot[j], 16)
                    sync.wait_ge(s_l1b, sf + 1)
                    wseq(sync, s_mix, 3 * sf)
                    for tb in range(TB):
                        sync.wait_ge(s_l1ap, sf * TB + tb + 1)
                        for ic in range(NCC):
                            sync.dma_start_transpose(
                                ln1_fm[:, ic,
                                       LP + tb * 128:LP + (tb + 1) * 128],
                                tm_stage[:, tb % 2, ic * 128:(ic + 1) * 128]
                            ).then_inc(s_x1f, 16)
                            x1f_cnt[0] += 16
                        x1f_done[(sf, tb)] = x1f_cnt[0]
                if it >= 0:
                    s = it
                    sync.wait_ge(s_omm, TB * (s + 1))
                    for tb in range(TB):
                        sync.wait_ge(s_l3ap, s * TB + tb + 1)
                        for ic in range(NCC):
                            sync.dma_start_transpose(
                                rwkv[:, ic, tb * 128:(tb + 1) * 128],
                                tm_stage[:, tb % 2, ic * 128:(ic + 1) * 128]
                            ).then_inc(s_x3f, 16)
                            x3f_cnt[0] += 16
                        x3f_done[(s, tb)] = x3f_cnt[0]
                    for tb in range(TB):
                        q = s * TB + tb
                        sync.wait_ge(s_out, q + 1)
                        sync.dma_start(out=out_h[q * 128:(q + 1) * 128, :],
                                       in_=out_r[:]
                                       ).then_inc(s_od, 16)
            sync.wait_ge(s_od, 16 * NQ)
            ndbg = 0
            for nm in DBG:
                sync.dma_start(out=dbg_out[nm], in_=tens[nm][:]
                               ).then_inc(s_cst, 16)
                ndbg += 16
            if ndbg:
                sync.wait_ge(s_cst, 160 + ndbg)

        @block.gpsimd
        def _(g):
            for wi, (nm, t) in enumerate((("wkT", wk), ("wvT", wv),
                                          ("wrT", wr), ("woT", wo))):
                wsrc = w_h[nm].rearrange("(a p) o -> p a o", p=128)
                g.dma_start(out=t[:], in_=wsrc).then_inc(s_wc[wi], 16)
            g.wait_ge(s_cst, 160)
            for it in range(-1, nsc):
                sf = it + 1
                if sf < nsc:
                    g.wait_ge(s_diff, sf + 1)
                    wseq(g, s_kvrmm, 24 * sf)
                    for mcol, xm in ((mkc, xmk), (mvc, xmv), (mrc, diff)):
                        for ic in range(NCC):
                            g.tensor_scalar(xm[:, ic, :], diff[:, ic, :],
                                            mcol[:, ic:ic + 1], None,
                                            ALU.mult).then_inc(s_mixp, 1)
                if it >= 0:
                    s = it
                    for tb in range(TB):
                        q = s * TB + tb
                        g.wait_ge(s_nb2, q + 1)
                        g.tensor_scalar(xres[:, tb, :], xres[:, tb, :],
                                        st2[:, 1, tb:tb + 1], None, ALU.add
                                        ).then_inc(s_xnb, 1)
                    for tb in range(TB):
                        q = s * TB + tb
                        g.wait_ge(s_nb4, q + 1)
                        g.wait_ge(s_l3ap, q + 1)
                        g.tensor_scalar(xp_v[:, tb, :], xp_v[:, tb, :],
                                        st4[:, 1, tb:tb + 1], None, ALU.add
                                        ).then_inc(s_xnb4, 1)
                if sf < nsc:
                    for tb in range(TB):
                        q = sf * TB + tb
                        g.wait_ge(s_slot[q % XR], xq_ready[q])
                        if q - TB >= 0:
                            g.wait_ge(s_xp, q - TB + 1)
                        g.tensor_copy(xres[:, tb, :], x_tm[:, q % XR, :]
                                      ).then_inc(s_xcp, 1)

        @block.tensor
        def _(tensor):
            tensor.wait_ge(s_cst, 160)
            tensor.wait_ge(s_lch, 1)
            tensor.matmul(pb_kvr[0][:, 0:NCC], big0[0:16, 0:128], lrhs[:],
                          start=True, stop=True).then_inc(s_lch, 1)
            tensor.wait_ge(s_lam, 2)
            kvr_fill = 0
            for s in range(nsc):
                for pi in range(3):
                    if pi == 2:
                        tensor.wait_ge(s_kvrev, 24 * s + 8)
                        wseq(tensor, s_bext, s)
                        for m in range(NCC):
                            mm = tensor.matmul(pb_b[:], hm16[:, m, :],
                                               ekri[:, m, :],
                                               start=(m == 0),
                                               stop=(m == NCC - 1))
                            if m == NCC - 1:
                                mm.then_inc(s_bmm, 1)
                    w_t = (wk, wv, wr)[pi]
                    xm_t = (xmk, xmv, diff)[pi]
                    if s == 0:
                        tensor.wait_ge(s_wc[pi], 16)
                    tensor.wait_ge(s_mix, 3 * s + pi + 1)
                    for m in range(NCC):
                        wseq(tensor, s_kvrev, kvr_fill - 1)
                        for kc in range(NCC):
                            mm = tensor.matmul(
                                pb_kvr[kvr_fill % 2][:],
                                w_t[:, kc, m * 128:(m + 1) * 128],
                                xm_t[:, kc, :],
                                start=(kc == 0), stop=(kc == NCC - 1))
                            if kc == NCC - 1:
                                mm.then_inc(s_kvrmm, 1)
                        kvr_fill += 1
                tensor.wait_ge(s_den, s + 1)
                for m in range(NCC):
                    wseq(tensor, s_rinv, 8 * s + m)
                    tensor.matmul(pb_r[:], sel2[:, m, :],
                                  den_so.bitcast(BF16)[:, 0:SC],
                                  start=True, stop=True).then_inc(s_rmm, 1)
                tensor.wait_ge(s_rwkv, s + 1)
                if s == 0:
                    tensor.wait_ge(s_wc[3], 16)
                for tb in range(TB):
                    pair_wait(tensor, "o", s, tb)
                    for cc in range(NCC):
                        last = cc == NCC - 1
                        tensor.matmul(pb_olo[tb % 2][:],
                                      rwkv[:, cc, tb * 128:(tb + 1) * 128],
                                      wo[:, cc, 0:512],
                                      start=(cc == 0), stop=last)
                        mm = tensor.matmul(
                            pb_ohi[tb % 2][:],
                            rwkv[:, cc, tb * 128:(tb + 1) * 128],
                            wo[:, cc, 512:1024],
                            start=(cc == 0), stop=last)
                        if last:
                            mm.then_inc(s_omm, 1)
                if s == 0:
                    tensor.wait_ge(s_wfsg, NCC)
                for tb in range(TB):
                    tensor.wait_ge(s_x3f, 16 * 32 * (s + 1))
                    pair_wait(tensor, "f", s, tb)
                    for cc in range(NCC):
                        last = cc == NCC - 1
                        tensor.matmul(pb_olo[tb % 2][:],
                                      rwkv[:, cc, tb * 128:(tb + 1) * 128],
                                      wf[:, cc, 0:512],
                                      start=(cc == 0), stop=last)
                        mm = tensor.matmul(
                            pb_ohi[tb % 2][:],
                            rwkv[:, cc, tb * 128:(tb + 1) * 128],
                            wf[:, cc, 512:1024],
                            start=(cc == 0), stop=last)
                        if last:
                            mm.then_inc(s_fmm, 1)

        @block.scalar
        def _(scalar):
            for ic in range(NCC):
                scalar.wait_ge(s_slot[ic % 2], wf_done[ic])
                scalar.activation(wf[:, ic, :], x_tm[:, ic % 2, :], AF.Sign
                                  ).then_inc(s_wfsg, 1)
            scalar.wait_ge(s_cst, 160)
            scalar.wait_ge(s_cmem, 1)
            scalar.activation(e16[:], tdc[:], AF.Exp)
            scalar.activation(lam16[:], e16[:], AF.Exp, scale=-1.0
                              ).then_inc(s_lam, 1)
            scalar.activation(eu16[:], tfc[:], AF.Exp)
            scalar.activation(eu_ch[:], ucs[:], AF.Exp)
            scalar.wait_ge(s_lch, 2)
            scalar.copy(lam_ch[:], pb_kvr[0][:, 0:NCC]).then_inc(s_lam, 1)
            for it in range(-1, nsc):
                sf = it + 1
                if sf < nsc:
                    for tb in range(TB):
                        q = sf * TB + tb
                        scalar.wait_ge(s_slot[q % XR], xq_ready[q])
                        scalar.activation(dump[:], x_tm[:, q % XR, 0:512],
                                          AF.Square,
                                          accum_out=st1[:, 2, tb:tb + 1])
                        scalar.activation(dump[:], x_tm[:, q % XR, 512:1024],
                                          AF.Square,
                                          accum_out=st1[:, 3, tb:tb + 1]
                                          ).then_inc(s_sq1, 1)
                    scalar.wait_ge(s_ch1, sf + 1)
                    scalar.activation(st1[:, 2, :], st1[:, 2, :], AF.Ln)
                    scalar.activation(st1[:, 3, :], st1[:, 2, :], AF.Exp,
                                      scale=-0.5).then_inc(s_le1, 1)
                    scalar.wait_ge(s_nb1, sf + 1)
                    for tb in range(TB):
                        q = sf * TB + tb
                        prev = tm_reader.get(tb % 2)
                        if prev is not None:
                            psem, pcnt = prev
                            scalar.wait_ge(psem, pcnt)
                        tm_reader[tb % 2] = (s_x1f, x1f_done[(sf, tb)])
                        scalar.activation(tm_stage[:, tb % 2, :],
                                          x_tm[:, q % XR, :], AF.Identity,
                                          bias=st1[:, 1, tb:tb + 1],
                                          scale=st1[:, 3, tb:tb + 1]
                                          ).then_inc(s_l1ap, 1)
                if it >= 0:
                    s = it
                    for m in range(NCC):
                        scalar.wait_ge(s_kvrmm, 24 * s + m + 1)
                        scalar.activation(ekri[:, m, :], pb_kvr[m % 2][:],
                                          AF.Exp).then_inc(s_kvrev, 1)
                    for m in range(NCC):
                        scalar.wait_ge(s_kvrmm, 24 * s + 8 + m + 1)
                        scalar.copy(vvnum[:, m, :], pb_kvr[m % 2][:]
                                    ).then_inc(s_kvrev, 1)
                    if s >= 1:
                        scalar.copy(b_ext[:, 0:1], b_ext[:, SC:SC + 1])
                    scalar.wait_ge(s_bmm, s + 1)
                    scalar.copy(b_ext[:, 1:SC + 1], pb_b[:]).then_inc(s_bext, 1)
                    scalar.wait_ge(s_aTT, s + 1)
                    for m in range(NCC):
                        scalar.wait_ge(s_kvrmm, 24 * s + 16 + m + 1)
                        scalar.activation(ekri[:, m, :], pb_kvr[m % 2][:],
                                          AF.Sigmoid).then_inc(s_kvrev, 1)
                    for tb in range(TB):
                        tau = s * TB + tb
                        scalar.wait_ge(s_omm, tau + 1)
                        scalar.activation(dump[:, 0:512], pb_olo[tb % 2][:],
                                          AF.Copy,
                                          accum_out=st2[:, 4, tb:tb + 1])
                        scalar.activation(dump[:, 0:512], pb_ohi[tb % 2][:],
                                          AF.Copy,
                                          accum_out=st2[:, 5, tb:tb + 1])
                        scalar.activation(dump[:, 0:512], pb_olo[tb % 2][:],
                                          AF.Square,
                                          accum_out=st2[:, 2, tb:tb + 1])
                        scalar.activation(dump[:, 0:512], pb_ohi[tb % 2][:],
                                          AF.Square,
                                          accum_out=st2[:, 3, tb:tb + 1]
                                          ).then_inc(s_ostat, 1)
                        scalar.wait_ge(s_ch2, tau + 1)
                        scalar.activation(st2[:, 2, tb:tb + 1],
                                          st2[:, 2, tb:tb + 1], AF.Ln)
                        scalar.activation(st2[:, 3, tb:tb + 1],
                                          st2[:, 2, tb:tb + 1], AF.Exp,
                                          scale=-0.5).then_inc(s_le2, 1)
                    for tb in range(TB):
                        tau = s * TB + tb
                        scalar.wait_ge(s_xp, tau + 1)
                        scalar.activation(dump[:], xp_v[:, tb, 0:512],
                                          AF.Square,
                                          accum_out=st3[:, 2, tb:tb + 1])
                        scalar.activation(dump[:], xp_v[:, tb, 512:1024],
                                          AF.Square,
                                          accum_out=st3[:, 3, tb:tb + 1]
                                          ).then_inc(s_sq3, 1)
                    scalar.wait_ge(s_ch3, s + 1)
                    scalar.activation(st3[:, 2, :], st3[:, 2, :], AF.Ln)
                    scalar.activation(st3[:, 3, :], st3[:, 2, :], AF.Exp,
                                      scale=-0.5).then_inc(s_le3, 1)
                    scalar.wait_ge(s_nb3, s + 1)
                    for tb in range(TB):
                        prev = tm_reader.get(tb % 2)
                        if prev is not None:
                            psem, pcnt = prev
                            scalar.wait_ge(psem, pcnt)
                        tm_reader[tb % 2] = (s_x3f, x3f_done[(s, tb)])
                        scalar.activation(tm_stage[:, tb % 2, :],
                                          xp_v[:, tb, :], AF.Identity,
                                          bias=st3[:, 1, tb:tb + 1],
                                          scale=st3[:, 3, tb:tb + 1]
                                          ).then_inc(s_l3ap, 1)
                    for tb in range(TB):
                        tau = s * TB + tb
                        scalar.wait_ge(s_fmm, tau + 1)
                        scalar.activation(dump[:, 0:512], pb_olo[tb % 2][:],
                                          AF.Copy,
                                          accum_out=st4[:, 4, tb:tb + 1])
                        scalar.activation(dump[:, 0:512], pb_ohi[tb % 2][:],
                                          AF.Copy,
                                          accum_out=st4[:, 5, tb:tb + 1])
                        scalar.activation(dump[:, 0:512], pb_olo[tb % 2][:],
                                          AF.Square,
                                          accum_out=st4[:, 2, tb:tb + 1])
                        scalar.activation(dump[:, 0:512], pb_ohi[tb % 2][:],
                                          AF.Square,
                                          accum_out=st4[:, 3, tb:tb + 1]
                                          ).then_inc(s_fstat, 1)
                        scalar.wait_ge(s_ch4, tau + 1)
                        scalar.activation(st4[:, 2, tb:tb + 1],
                                          st4[:, 2, tb:tb + 1], AF.Ln)
                        scalar.activation(st4[:, 3, tb:tb + 1],
                                          st4[:, 2, tb:tb + 1], AF.Exp,
                                          scale=-0.5).then_inc(s_le4, 1)

        @block.vector
        def _(vector):
            vector.memset(consts[:, 0:1], -0.5)
            vector.memset(consts[:, 1:2], -1.0).then_inc(s_cmem, 1)
            vector.memset(carry[:], 0.0)
            vector.memset(dcar[:], 0.0)
            vector.memset(a_ext[:, :, 0:1], 0.0)
            vector.memset(ln1_fm[:, :, LP - 1:LP], 0.0)
            vector.memset(b_ext[:, 0:1], 0.0)
            vector.wait_ge(s_cst, 160)
            vector.wait_ge(s_lam, 1)
            vector.tensor_tensor(lrhs[:], lam16[:].broadcast_to([H, NCC]),
                                 hm8[:], ALU.mult).then_inc(s_lch, 1)
            vector.wait_ge(s_lam, 2)
            for it in range(-1, nsc):
                sf = it + 1
                if sf < nsc:
                    if sf >= 1:
                        vector.tensor_copy(ln1_fm[:, :, LP - 1:LP],
                                           ln1_fm[:, :, SC + LP - 1:SC + LP])
                    vector.sem_inc(s_l1b, 1)
                    for tb in range(TB):
                        q = sf * TB + tb
                        vector.wait_ge(s_slot[q % XR], xq_ready[q])
                        vector.reduce_sum(st1[:, 4, tb:tb + 1],
                                          x_tm[:, q % XR, :],
                                          mybir.AxisListType.X)
                    vector.wait_ge(s_sq1, TB * (sf + 1))
                    vector.tensor_tensor(st1[:, 5, :], st1[:, 2, :],
                                         st1[:, 3, :], ALU.add)
                    vector.tensor_scalar(st1[:, 0, :], st1[:, 4, :],
                                         1.0 / C, None, ALU.mult)
                    vector.tensor_scalar(st1[:, 2, :], st1[:, 5, :],
                                         1.0 / C, EPS, ALU.mult, ALU.add)
                    vsync(vector, vector.tensor_tensor(
                        st1[:, 1, :], st1[:, 0, :], st1[:, 0, :], ALU.mult))
                    vector.tensor_tensor(st1[:, 2, :], st1[:, 2, :],
                                         st1[:, 1, :], ALU.subtract
                                         ).then_inc(s_ch1, 1)
                    vector.wait_ge(s_le1, sf + 1)
                    vector.scalar_tensor_tensor(st1[:, 1, :], st1[:, 0, :],
                                                -1.0, st1[:, 3, :],
                                                ALU.mult, ALU.mult
                                                ).then_inc(s_nb1, 1)
                    if it == -1:
                        vector.wait_ge(s_x1f, 16 * 32 * (sf + 1))
                        vector.tensor_tensor(diff[:],
                                             ln1_fm[:, :, LP:SC + LP],
                                             ln1_fm[:, :, LP - 1:SC + LP - 1],
                                             ALU.subtract).then_inc(s_diff, 1)
                        for pi, xm in enumerate((xmk, xmv, diff)):
                            vector.wait_ge(s_mixp, 8 * (pi + 1))
                            vector.tensor_tensor(
                                xm[:], xm[:],
                                ln1_fm[:, :, LP - 1:SC + LP - 1], ALU.add
                            ).then_inc(s_mix, 1)
                if it >= 0:
                    s = it
                    if s >= 1:
                        vector.tensor_copy(a_ext[:, :, 0:1],
                                           a_ext[:, :, SC:SC + 1])
                    vector.wait_ge(s_kvrev, 24 * s + 16)
                    vector.tensor_tensor(a_ext[:, :, 1:SC + 1], ekri[:],
                                         vvnum[:], ALU.mult
                                         ).then_inc(s_aTT, 1)
                    wseq(vector, s_l3ap, TB * s)
                    wseq(vector, s_xnb4, TB * s)
                    for m in range(NCC):
                        sc_i = vector.tensor_tensor_scan(
                            scan_v[:, m, :],
                            lam_ch[:, m:m + 1].broadcast_to([128, SC]),
                            a_ext[:, m, 0:SC], carry[:, m:m + 1],
                            ALU.mult, ALU.add)
                    vsync(vector, sc_i)
                    vector.tensor_copy(carry[:], scan_v[:, :, SC - 1])
                    vector.wait_ge(s_bext, s + 1)
                    dsc_i = vector.tensor_tensor_scan(
                        den_so, lam16[:].broadcast_to([H, SC]),
                        b_ext[:, 0:SC], dcar[:], ALU.mult, ALU.add)
                    vsync(vector, dsc_i)
                    vector.tensor_copy(dcar[:], den_so[:, SC - 1:SC])
                    vector.tensor_scalar(b_ext[:, 0:SC], b_ext[:, 1:SC + 1],
                                         eu16[:], 1e-6, ALU.mult, ALU.max)
                    vector.tensor_tensor(den_so, den_so,
                                         b_ext[:, 0:SC], ALU.add)
                    vector.reciprocal(b_ext[:, 0:SC], den_so)
                    vector.tensor_copy(den_so.bitcast(BF16)[:, 0:SC],
                                       b_ext[:, 0:SC]).then_inc(s_den, 1)
                    for m in range(NCC):
                        vector.scalar_tensor_tensor(
                            vvnum[:, m, :], a_ext[:, m, 1:SC + 1],
                            eu_ch[:, m:m + 1], scan_v[:, m, :],
                            ALU.mult, ALU.add)
                    for m in range(NCC):
                        vector.wait_ge(s_kvrev, 24 * s + 16 + m + 1)
                        vector.wait_ge(s_rmm, 8 * s + m + 1)
                        vector.tensor_tensor(ekri[:, m, :], ekri[:, m, :],
                                             pb_r[:], ALU.mult
                                             ).then_inc(s_rinv, 1)
                    wseq(vector, s_fmm, TB * s)
                    vector.tensor_tensor(rwkv[:], vvnum[:], ekri[:], ALU.mult
                                         ).then_inc(s_rwkv, 1)
                    if s + 1 < nsc:
                        vector.wait_ge(s_x1f, 16 * 32 * (s + 2))
                        vector.wait_ge(s_mixp, 24 * (s + 1))
                        vector.wait_ge(s_kvrmm, 24 * (s + 1))
                        vector.tensor_tensor(diff[:],
                                             ln1_fm[:, :, LP:SC + LP],
                                             ln1_fm[:, :, LP - 1:SC + LP - 1],
                                             ALU.subtract).then_inc(s_diff, 1)
                        for pi, xm in enumerate((xmk, xmv, diff)):
                            vector.wait_ge(s_mixp,
                                           24 * (s + 1) + 8 * (pi + 1))
                            vector.tensor_tensor(
                                xm[:], xm[:],
                                ln1_fm[:, :, LP - 1:SC + LP - 1], ALU.add
                            ).then_inc(s_mix, 1)
                    for tb in range(TB):
                        tau = s * TB + tb
                        q = tau
                        vector.wait_ge(s_ostat, tau + 1)
                        vector.tensor_tensor(st2[:, 4, tb:tb + 1],
                                             st2[:, 4, tb:tb + 1],
                                             st2[:, 5, tb:tb + 1], ALU.add)
                        vector.tensor_tensor(st2[:, 5, tb:tb + 1],
                                             st2[:, 2, tb:tb + 1],
                                             st2[:, 3, tb:tb + 1], ALU.add)
                        vector.tensor_scalar(st2[:, 0, tb:tb + 1],
                                             st2[:, 4, tb:tb + 1],
                                             1.0 / C, None, ALU.mult)
                        vector.tensor_scalar(st2[:, 2, tb:tb + 1],
                                             st2[:, 5, tb:tb + 1],
                                             1.0 / C, EPS, ALU.mult, ALU.add)
                        vsync(vector, vector.tensor_tensor(
                            st2[:, 1, tb:tb + 1], st2[:, 0, tb:tb + 1],
                            st2[:, 0, tb:tb + 1], ALU.mult))
                        vector.tensor_tensor(st2[:, 2, tb:tb + 1],
                                             st2[:, 2, tb:tb + 1],
                                             st2[:, 1, tb:tb + 1],
                                             ALU.subtract).then_inc(s_ch2, 1)
                        vector.wait_ge(s_le2, tau + 1)
                        vector.scalar_tensor_tensor(st2[:, 1, tb:tb + 1],
                                                    st2[:, 0, tb:tb + 1],
                                                    -1.0,
                                                    st2[:, 3, tb:tb + 1],
                                                    ALU.mult, ALU.mult
                                                    ).then_inc(s_nb2, 1)
                        vector.wait_ge(s_xnb, tau + 1)
                        vector.scalar_tensor_tensor(
                            xp_v[:, tb, 0:512], pb_olo[tb % 2][:],
                            st2[:, 3, tb:tb + 1], xres[:, tb, 0:512],
                            ALU.mult, ALU.add)
                        vector.scalar_tensor_tensor(
                            xp_v[:, tb, 512:1024], pb_ohi[tb % 2][:],
                            st2[:, 3, tb:tb + 1], xres[:, tb, 512:1024],
                            ALU.mult, ALU.add).then_inc(s_xp, 1)
                        vector.reduce_sum(st3[:, 4, tb:tb + 1],
                                          xp_v[:, tb, :],
                                          mybir.AxisListType.X)
                    vector.wait_ge(s_sq3, TB * (s + 1))
                    vector.tensor_tensor(st3[:, 5, :], st3[:, 2, :],
                                         st3[:, 3, :], ALU.add)
                    vector.tensor_scalar(st3[:, 0, :], st3[:, 4, :],
                                         1.0 / C, None, ALU.mult)
                    vector.tensor_scalar(st3[:, 2, :], st3[:, 5, :],
                                         1.0 / C, EPS, ALU.mult, ALU.add)
                    vsync(vector, vector.tensor_tensor(
                        st3[:, 1, :], st3[:, 0, :], st3[:, 0, :], ALU.mult))
                    vector.tensor_tensor(st3[:, 2, :], st3[:, 2, :],
                                         st3[:, 1, :], ALU.subtract
                                         ).then_inc(s_ch3, 1)
                    vector.wait_ge(s_le3, s + 1)
                    vector.scalar_tensor_tensor(st3[:, 1, :], st3[:, 0, :],
                                                -1.0, st3[:, 3, :],
                                                ALU.mult, ALU.mult
                                                ).then_inc(s_nb3, 1)
                    for tb in range(TB):
                        tau = s * TB + tb
                        vector.wait_ge(s_fstat, tau + 1)
                        vector.tensor_tensor(st4[:, 4, tb:tb + 1],
                                             st4[:, 4, tb:tb + 1],
                                             st4[:, 5, tb:tb + 1], ALU.add)
                        vector.tensor_tensor(st4[:, 5, tb:tb + 1],
                                             st4[:, 2, tb:tb + 1],
                                             st4[:, 3, tb:tb + 1], ALU.add)
                        vector.tensor_scalar(st4[:, 0, tb:tb + 1],
                                             st4[:, 4, tb:tb + 1],
                                             1.0 / C, None, ALU.mult)
                        vector.tensor_scalar(st4[:, 2, tb:tb + 1],
                                             st4[:, 5, tb:tb + 1],
                                             1.0 / C, EPS, ALU.mult, ALU.add)
                        vsync(vector, vector.tensor_tensor(
                            st4[:, 1, tb:tb + 1], st4[:, 0, tb:tb + 1],
                            st4[:, 0, tb:tb + 1], ALU.mult))
                        vector.tensor_tensor(st4[:, 2, tb:tb + 1],
                                             st4[:, 2, tb:tb + 1],
                                             st4[:, 1, tb:tb + 1],
                                             ALU.subtract).then_inc(s_ch4, 1)
                        vector.wait_ge(s_le4, tau + 1)
                        vector.scalar_tensor_tensor(st4[:, 1, tb:tb + 1],
                                                    st4[:, 0, tb:tb + 1],
                                                    -1.0,
                                                    st4[:, 3, tb:tb + 1],
                                                    ALU.mult, ALU.mult
                                                    ).then_inc(s_nb4, 1)
                        vector.wait_ge(s_xnb4, tau + 1)
                        wseq(vector, s_od, 16 * tau)
                        vector.scalar_tensor_tensor(
                            out_r[:, 0:512], pb_olo[tb % 2][:],
                            st4[:, 3, tb:tb + 1], xp_v[:, tb, 0:512],
                            ALU.mult, ALU.add)
                        vector.scalar_tensor_tensor(
                            out_r[:, 512:1024], pb_ohi[tb % 2][:],
                            st4[:, 3, tb:tb + 1], xp_v[:, tb, 512:1024],
                            ALU.mult, ALU.add).then_inc(s_out, 1)
    return nc


def kernel(L0_attn_x, L0_attn_mix_k, L0_attn_mix_v, L0_attn_mix_r,
           L0_attn_Wk, L0_attn_Wv, L0_attn_Wr, L0_attn_Wo,
           L0_attn_time_decay, L0_attn_time_first, L0_ffn_weight):
    B, T, _ = L0_attn_x.shape
    nsc = T // SC
    x = np.ascontiguousarray(np.asarray(L0_attn_x, dtype=np.float32))

    def wT(w):
        return np.ascontiguousarray(np.asarray(w, dtype=np.float32).T)

    def cols(v):
        return np.ascontiguousarray(
            np.asarray(v, dtype=np.float32).reshape(NCC, 128).T)

    tf = np.asarray(L0_attn_time_first, dtype=np.float32)
    td = np.asarray(L0_attn_time_decay, dtype=np.float32)
    u_cols = cols(np.repeat(tf, C // H))

    p_ge64 = (np.arange(128) >= 64).astype(np.int64)
    parity = ((np.arange(H)[:, None] % 2) == p_ge64[None, :]
              ).astype(np.float32)
    hm8 = (np.arange(H)[:, None] // 2 == np.arange(NCC)[None, :]
           ).astype(np.float32)
    head_of = 2 * np.arange(NCC)[:, None, None] + p_ge64[None, :, None]
    hm16 = (np.arange(H)[None, None, :] == head_of).astype(bf16)
    sel2 = np.ascontiguousarray(np.transpose(hm16, (0, 2, 1)))

    shared = {
        "wkT": wT(L0_attn_Wk), "wvT": wT(L0_attn_Wv), "wrT": wT(L0_attn_Wr),
        "woT": wT(L0_attn_Wo), "wfT": wT(L0_ffn_weight),
        "mkc": cols(np.asarray(L0_attn_mix_k).reshape(-1)),
        "mvc": cols(np.asarray(L0_attn_mix_v).reshape(-1)),
        "mrc": cols(np.asarray(L0_attn_mix_r).reshape(-1)),
        "ucols": u_cols,
        "tdcol": np.ascontiguousarray(td.reshape(H, 1)),
        "tfcol": np.ascontiguousarray(tf.reshape(H, 1)),
        "parity": parity, "hm8": hm8,
        "hm16": np.ascontiguousarray(hm16),
        "sel2": sel2,
    }
    nc = build_program(nsc)
    in_maps = [dict(shared, x=np.ascontiguousarray(x[b])) for b in range(B)]
    res = run_bass_kernel_spmd(nc, in_maps, core_ids=list(range(B)))
    return np.stack([np.asarray(r["out"], dtype=np.float32)
                     for r in res.results], axis=0)


# version 02
# speedup vs baseline: 2.1690x; 2.1690x over previous
import numpy as np
import ml_dtypes
import concourse.bass as bass
import concourse.mybir as mybir
from concourse.bass_utils import run_bass_kernel_spmd
from contextlib import ExitStack

F32 = mybir.dt.float32
BF16 = mybir.dt.bfloat16
AF = mybir.ActivationFunctionType
ALU = mybir.AluOpType
bf16 = ml_dtypes.bfloat16

DBG = []

C = 1024
H = 16
NCC = 8
SC = 512
TB = 4
LP = 16
XR = 5
EPS = 1e-5


def build_program(nsc):
    T = nsc * SC
    NQ = nsc * TB
    nc = bass.Bass("TRN2", target_bir_lowering=False, debug=False,
                   detect_race_conditions=False)

    x_h = nc.dram_tensor("x", [T, C], F32, kind="ExternalInput").ap()
    w_h = {}
    for nm in ("wkT", "wvT", "wrT", "woT", "wfT"):
        w_h[nm] = nc.dram_tensor(nm, [C, C], F32, kind="ExternalInput").ap()
    mk_h = nc.dram_tensor("mkc", [128, NCC], F32, kind="ExternalInput").ap()
    mv_h = nc.dram_tensor("mvc", [128, NCC], F32, kind="ExternalInput").ap()
    mr_h = nc.dram_tensor("mrc", [128, NCC], F32, kind="ExternalInput").ap()
    uc_h = nc.dram_tensor("ucols", [128, NCC], F32, kind="ExternalInput").ap()
    td_h = nc.dram_tensor("tdcol", [H, 1], F32, kind="ExternalInput").ap()
    tf_h = nc.dram_tensor("tfcol", [H, 1], F32, kind="ExternalInput").ap()
    par_h = nc.dram_tensor("parity", [H, 128], F32, kind="ExternalInput").ap()
    hm8_h = nc.dram_tensor("hm8", [H, NCC], F32, kind="ExternalInput").ap()
    hm16_h = nc.dram_tensor("hm16", [NCC, 128, H], BF16, kind="ExternalInput").ap()
    sel2_h = nc.dram_tensor("sel2", [NCC, H, 128], BF16, kind="ExternalInput").ap()
    out_h = nc.dram_tensor("out", [T, C], F32, kind="ExternalOutput").ap()

    es = ExitStack()

    def sb(name, shape, dt):
        return es.enter_context(nc.sbuf_tensor(name, shape, dt))

    def ps(name, shape):
        return es.enter_context(nc.psum_tensor(name, shape, F32))

    def sem(name):
        return es.enter_context(nc.semaphore(name))

    with es:
        wk = sb("wk", [128, NCC, C], BF16)
        wv = sb("wv", [128, NCC, C], BF16)
        wr = sb("wr", [128, NCC, C], BF16)
        wo = sb("wo", [128, NCC, C], BF16)
        wf = sb("wf", [128, NCC, C], BF16)
        x_tm = sb("x_tm", [128, XR, C], F32)
        tm_stage = sb("tm_stage", [128, 2, C], BF16)
        ln1_fm = sb("ln1_fm", [128, NCC, SC + LP], BF16)
        diff = sb("diff", [128, NCC, SC], BF16)
        xmk = sb("xmk", [128, NCC, SC], BF16)
        xmv = sb("xmv", [128, NCC, SC], BF16)
        ekri = sb("ekri", [128, NCC, SC], BF16)
        vvnum = sb("vvnum", [128, NCC, SC], BF16)
        xres = sb("xres", [128, TB, C], BF16)
        a_ext = sb("a_ext", [128, NCC, SC + 1], BF16)
        big0 = sb("big0", [128, TB * C], F32)
        rwkv = sb("rwkv", [128, NCC, SC], BF16)
        out_r = sb("out_r", [128, C], F32)
        dump = sb("dump", [128, 512], BF16)
        lam_ch = sb("lam_ch", [128, NCC], F32)
        eu_ch = sb("eu_ch", [128, NCC], F32)
        mkc = sb("mkc_s", [128, NCC], F32)
        mvc = sb("mvc_s", [128, NCC], F32)
        mrc = sb("mrc_s", [128, NCC], F32)
        ucs = sb("ucs", [128, NCC], F32)
        carry = sb("carry", [128, NCC], F32)
        tdc = sb("tdc", [H, 1], F32)
        tfc = sb("tfc", [H, 1], F32)
        e16 = sb("e16", [H, 1], F32)
        lam16 = sb("lam16", [H, 1], F32)
        eu16 = sb("eu16", [H, 1], F32)
        lrhs = sb("lrhs", [H, NCC], F32)
        hm8 = sb("hm8_s", [H, NCC], F32)
        hm16 = sb("hm16_s", [128, NCC, H], BF16)
        sel2 = sb("sel2_s", [H, NCC, 128], BF16)
        b_ext_t = sb("b_ext", [H, SC + 1], F32)
        den_so_t = sb("den_so", [H, SC], F32)
        b_ext = b_ext_t[:]
        den_so = den_so_t[:]
        dcar = sb("dcar", [H, 1], F32)
        consts = sb("consts", [128, 2], F32)
        st1 = sb("st1", [128, 6, TB], F32)
        st2 = sb("st2", [128, 6, TB], F32)
        st3 = sb("st3", [128, 6, TB], F32)
        st4 = sb("st4", [128, 6, TB], F32)

        tens = dict(wk=wk, wv=wv, wr=wr, wo=wo, wf=wf, x_tm=x_tm,
                    tm_stage=tm_stage, ln1_fm=ln1_fm, diff=diff, xmk=xmk,
                    xmv=xmv, ekri=ekri, vvnum=vvnum, xres=xres, a_ext=a_ext,
                    big0=big0, rwkv=rwkv, out_r=out_r, dump=dump,
                    lam_ch=lam_ch, eu_ch=eu_ch, carry=carry, lam16=lam16,
                    eu16=eu16, b_ext=b_ext_t, den_so=den_so_t, dcar=dcar,
                    st1=st1, st2=st2, st3=st3, st4=st4, hm16=hm16, sel2=sel2)
        dbg_out = {}
        for nm in DBG:
            t = tens[nm]
            dbg_out[nm] = nc.dram_tensor(
                "dbg_" + nm, list(t[:].shape), t[:].dtype,
                kind="ExternalOutput").ap()

        pb_kvr = [ps("pb_kvr0", [128, 512]), ps("pb_kvr1", [128, 512])]
        pb_olo = [ps("pb_olo0", [128, 512]), ps("pb_olo1", [128, 512])]
        pb_ohi = [ps("pb_ohi0", [128, 512]), ps("pb_ohi1", [128, 512])]
        pb_b = ps("pb_b", [H, 512])
        pb_r = ps("pb_r", [128, 512])

        s_cst = sem("s_cst")
        s_wc = [sem(f"s_wc{i}") for i in range(4)]
        s_slot = [sem(f"s_slot{j}") for j in range(XR)]
        slot_exp = [0] * XR
        xq_ready = {}
        outd_done = {}
        wf_done = {}
        x1f_cnt = [0]
        x3f_cnt = [0]
        x1f_done = {}
        x3f_done = {}
        tm_reader = {}
        s_wfsg = sem("s_wfsg")
        s_cmem = sem("s_cmem")
        s_lam = sem("s_lam")
        s_lch = sem("s_lch")
        s_l1b = sem("s_l1b")
        s_l1ap = sem("s_l1ap")
        s_x1f = sem("s_x1f")
        s_diff = sem("s_diff")
        s_sq1 = sem("s_sq1")
        s_ch1 = sem("s_ch1")
        s_le1 = sem("s_le1")
        s_nb1 = sem("s_nb1")
        s_mixp = sem("s_mixp")
        s_mix = sem("s_mix")
        s_kvrmm = sem("s_kvrmm")
        s_kvrev = sem("s_kvrev")
        s_bmm = sem("s_bmm")
        s_bext = sem("s_bext")
        s_den = sem("s_den")
        s_rmm = sem("s_rmm")
        s_rinv = sem("s_rinv")
        s_rwkv = sem("s_rwkv")
        s_aTT = sem("s_aTT")
        s_vx = sem("s_vx")
        s_xcp = sem("s_xcp")
        s_omm = sem("s_omm")
        s_ostat = sem("s_ostat")
        s_ch2 = sem("s_ch2")
        s_le2 = sem("s_le2")
        s_nb2 = sem("s_nb2")
        s_xnb = sem("s_xnb")
        s_xp = sem("s_xp")
        s_sq3 = sem("s_sq3")
        s_ch3 = sem("s_ch3")
        s_le3 = sem("s_le3")
        s_nb3 = sem("s_nb3")
        s_l3ap = sem("s_l3ap")
        s_x3f = sem("s_x3f")
        s_fmm = sem("s_fmm")
        s_fstat = sem("s_fstat")
        s_ch4 = sem("s_ch4")
        s_le4 = sem("s_le4")
        s_nb4 = sem("s_nb4")
        s_xnb4 = sem("s_xnb4")
        s_out = sem("s_out")
        s_od = sem("s_od")

        nc.const_aps.register(F32, -0.5, consts[:, 0:1])
        nc.const_aps.register(F32, -1.0, consts[:, 1:2])

        block = es.enter_context(nc.Block())

        vx_n = [0]

        def vsync(vector, inst):
            vx_n[0] += 1
            inst.then_inc(s_vx, 1)
            vector.wait_ge(s_vx, vx_n[0])

        scan_v = big0[:].rearrange("p (a b) -> p a b", a=NCC)
        xp_v = big0[:].rearrange("p (a b) -> p a b", a=TB)

        def wseq(eng, semh, val):
            if val > 0:
                eng.wait_ge(semh, val)

        pair_usages = []

        def pair_wait(tensor, kind, s, tb):
            pair_usages.append((kind, s, tb))
            g = len(pair_usages) - 1
            if g >= 2:
                k2, s2, tb2 = pair_usages[g - 2]
                tau2 = s2 * TB + tb2
                if k2 == "o":
                    tensor.wait_ge(s_ostat, tau2 + 1)
                    tensor.wait_ge(s_xp, tau2 + 1)
                else:
                    tensor.wait_ge(s_fstat, tau2 + 1)
                    tensor.wait_ge(s_out, tau2 + 1)

        @block.sync
        def _(sync):
            for ap_h, t in ((mk_h, mkc), (mv_h, mvc), (mr_h, mrc),
                            (uc_h, ucs), (td_h, tdc), (tf_h, tfc),
                            (hm8_h, hm8)):
                sync.dma_start(out=t[:], in_=ap_h).then_inc(s_cst, 16)
            sync.dma_start(out=big0[0:16, 0:128], in_=par_h
                           ).then_inc(s_cst, 16)
            sync.dma_start(out=hm16[:], in_=hm16_h.transpose([1, 0, 2])
                           ).then_inc(s_cst, 16)
            sync.dma_start(out=sel2[:], in_=sel2_h.transpose([1, 0, 2])
                           ).then_inc(s_cst, 16)
            for ic in range(NCC):
                if ic >= 2:
                    sync.wait_ge(s_wfsg, ic - 1)
                j = ic % 2
                slot_exp[j] += 16
                wf_done[ic] = slot_exp[j]
                sync.dma_start(out=x_tm[:, j, :],
                               in_=w_h["wfT"][ic * 128:(ic + 1) * 128, :]
                               ).then_inc(s_slot[j], 16)
            sync.wait_ge(s_wfsg, NCC)
            for it in range(-1, nsc):
                sf = it + 1
                if sf < nsc:
                    for tb in range(TB):
                        q = sf * TB + tb
                        j = q % XR
                        wseq(sync, s_l1ap, q - XR + 1)
                        wseq(sync, s_xcp, q - XR + 1)
                        if q - XR >= 0:
                            sync.wait_ge(s_nb1, (q - XR) // TB + 1)
                        slot_exp[j] += 16
                        xq_ready[q] = slot_exp[j]
                        sync.dma_start(out=x_tm[:, j, :],
                                       in_=x_h[q * 128:(q + 1) * 128, :]
                                       ).then_inc(s_slot[j], 16)
                    sync.wait_ge(s_l1b, sf + 1)
                    wseq(sync, s_mix, 3 * sf)
                    for tb in range(TB):
                        sync.wait_ge(s_l1ap, sf * TB + tb + 1)
                        for ic in range(NCC):
                            sync.dma_start_transpose(
                                ln1_fm[:, ic,
                                       LP + tb * 128:LP + (tb + 1) * 128],
                                tm_stage[:, tb % 2, ic * 128:(ic + 1) * 128]
                            ).then_inc(s_x1f, 16)
                            x1f_cnt[0] += 16
                        x1f_done[(sf, tb)] = x1f_cnt[0]
                if it >= 0:
                    s = it
                    sync.wait_ge(s_omm, TB * (s + 1))
                    for tb in range(TB):
                        sync.wait_ge(s_l3ap, s * TB + tb + 1)
                        for ic in range(NCC):
                            sync.dma_start_transpose(
                                rwkv[:, ic, tb * 128:(tb + 1) * 128],
                                tm_stage[:, tb % 2, ic * 128:(ic + 1) * 128]
                            ).then_inc(s_x3f, 16)
                            x3f_cnt[0] += 16
                        x3f_done[(s, tb)] = x3f_cnt[0]
                    for tb in range(TB):
                        q = s * TB + tb
                        sync.wait_ge(s_out, q + 1)
                        sync.dma_start(out=out_h[q * 128:(q + 1) * 128, :],
                                       in_=out_r[:]
                                       ).then_inc(s_od, 16)
            sync.wait_ge(s_od, 16 * NQ)
            ndbg = 0
            for nm in DBG:
                sync.dma_start(out=dbg_out[nm], in_=tens[nm][:]
                               ).then_inc(s_cst, 16)
                ndbg += 16
            if ndbg:
                sync.wait_ge(s_cst, 160 + ndbg)

        @block.gpsimd
        def _(g):
            for wi, (nm, t) in enumerate((("wkT", wk), ("wvT", wv),
                                          ("wrT", wr), ("woT", wo))):
                wsrc = w_h[nm].rearrange("(a p) o -> p a o", p=128)
                g.dma_start(out=t[:], in_=wsrc).then_inc(s_wc[wi], 16)
            g.wait_ge(s_cst, 160)

        @block.tensor
        def _(tensor):
            tensor.wait_ge(s_cst, 160)
            tensor.wait_ge(s_lch, 1)
            tensor.matmul(pb_kvr[0][:, 0:NCC], big0[0:16, 0:128], lrhs[:],
                          start=True, stop=True).then_inc(s_lch, 1)
            tensor.wait_ge(s_lam, 2)
            kvr_fill = 0
            for s in range(nsc):
                for pi in range(3):
                    if pi == 2:
                        tensor.wait_ge(s_kvrev, 24 * s + 8)
                        wseq(tensor, s_bext, s)
                        for m in range(NCC):
                            mm = tensor.matmul(pb_b[:], hm16[:, m, :],
                                               ekri[:, m, :],
                                               start=(m == 0),
                                               stop=(m == NCC - 1))
                            if m == NCC - 1:
                                mm.then_inc(s_bmm, 1)
                    w_t = (wk, wv, wr)[pi]
                    xm_t = (xmk, xmv, diff)[pi]
                    if s == 0:
                        tensor.wait_ge(s_wc[pi], 16)
                    tensor.wait_ge(s_mix, 3 * s + pi + 1)
                    for m in range(NCC):
                        wseq(tensor, s_kvrev, kvr_fill - 1)
                        for kc in range(NCC):
                            mm = tensor.matmul(
                                pb_kvr[kvr_fill % 2][:],
                                w_t[:, kc, m * 128:(m + 1) * 128],
                                xm_t[:, kc, :],
                                start=(kc == 0), stop=(kc == NCC - 1))
                            if kc == NCC - 1:
                                mm.then_inc(s_kvrmm, 1)
                        kvr_fill += 1
                tensor.wait_ge(s_den, s + 1)
                for m in range(NCC):
                    wseq(tensor, s_rinv, 8 * s + m)
                    tensor.matmul(pb_r[:], sel2[:, m, :],
                                  den_so.bitcast(BF16)[:, 0:SC],
                                  start=True, stop=True).then_inc(s_rmm, 1)
                tensor.wait_ge(s_rwkv, s + 1)
                if s == 0:
                    tensor.wait_ge(s_wc[3], 16)
                for tb in range(TB):
                    pair_wait(tensor, "o", s, tb)
                    for cc in range(NCC):
                        last = cc == NCC - 1
                        tensor.matmul(pb_olo[tb % 2][:],
                                      rwkv[:, cc, tb * 128:(tb + 1) * 128],
                                      wo[:, cc, 0:512],
                                      start=(cc == 0), stop=last)
                        mm = tensor.matmul(
                            pb_ohi[tb % 2][:],
                            rwkv[:, cc, tb * 128:(tb + 1) * 128],
                            wo[:, cc, 512:1024],
                            start=(cc == 0), stop=last)
                        if last:
                            mm.then_inc(s_omm, 1)
                if s == 0:
                    tensor.wait_ge(s_wfsg, NCC)
                for tb in range(TB):
                    tensor.wait_ge(s_x3f, 16 * 32 * (s + 1))
                    pair_wait(tensor, "f", s, tb)
                    for cc in range(NCC):
                        last = cc == NCC - 1
                        tensor.matmul(pb_olo[tb % 2][:],
                                      rwkv[:, cc, tb * 128:(tb + 1) * 128],
                                      wf[:, cc, 0:512],
                                      start=(cc == 0), stop=last)
                        mm = tensor.matmul(
                            pb_ohi[tb % 2][:],
                            rwkv[:, cc, tb * 128:(tb + 1) * 128],
                            wf[:, cc, 512:1024],
                            start=(cc == 0), stop=last)
                        if last:
                            mm.then_inc(s_fmm, 1)

        @block.scalar
        def _(scalar):
            for ic in range(NCC):
                scalar.wait_ge(s_slot[ic % 2], wf_done[ic])
                scalar.activation(wf[:, ic, :], x_tm[:, ic % 2, :], AF.Sign
                                  ).then_inc(s_wfsg, 1)
            scalar.wait_ge(s_cst, 160)
            scalar.wait_ge(s_cmem, 1)
            scalar.activation(e16[:], tdc[:], AF.Exp)
            scalar.activation(lam16[:], e16[:], AF.Exp, scale=-1.0
                              ).then_inc(s_lam, 1)
            scalar.activation(eu16[:], tfc[:], AF.Exp)
            scalar.activation(eu_ch[:], ucs[:], AF.Exp)
            scalar.wait_ge(s_lch, 2)
            scalar.copy(lam_ch[:], pb_kvr[0][:, 0:NCC]).then_inc(s_lam, 1)
            for it in range(-1, nsc):
                sf = it + 1
                if sf < nsc:
                    for tb in range(TB):
                        q = sf * TB + tb
                        scalar.wait_ge(s_slot[q % XR], xq_ready[q])
                        scalar.activation(dump[:], x_tm[:, q % XR, 0:512],
                                          AF.Square,
                                          accum_out=st1[:, 2, tb:tb + 1])
                        scalar.activation(dump[:], x_tm[:, q % XR, 512:1024],
                                          AF.Square,
                                          accum_out=st1[:, 3, tb:tb + 1]
                                          ).then_inc(s_sq1, 1)
                    scalar.wait_ge(s_ch1, sf + 1)
                    scalar.activation(st1[:, 2, :], st1[:, 2, :], AF.Ln)
                    scalar.activation(st1[:, 3, :], st1[:, 2, :], AF.Exp,
                                      scale=-0.5).then_inc(s_le1, 1)
                    scalar.wait_ge(s_nb1, sf + 1)
                    for tb in range(TB):
                        q = sf * TB + tb
                        prev = tm_reader.get(tb % 2)
                        if prev is not None:
                            psem, pcnt = prev
                            scalar.wait_ge(psem, pcnt)
                        tm_reader[tb % 2] = (s_x1f, x1f_done[(sf, tb)])
                        scalar.activation(tm_stage[:, tb % 2, :],
                                          x_tm[:, q % XR, :], AF.Identity,
                                          bias=st1[:, 1, tb:tb + 1],
                                          scale=st1[:, 3, tb:tb + 1]
                                          ).then_inc(s_l1ap, 1)
                if it >= 0:
                    s = it
                    for m in range(NCC):
                        scalar.wait_ge(s_kvrmm, 24 * s + m + 1)
                        scalar.activation(ekri[:, m, :], pb_kvr[m % 2][:],
                                          AF.Exp).then_inc(s_kvrev, 1)
                    for m in range(NCC):
                        scalar.wait_ge(s_kvrmm, 24 * s + 8 + m + 1)
                        scalar.copy(vvnum[:, m, :], pb_kvr[m % 2][:]
                                    ).then_inc(s_kvrev, 1)
                    if s >= 1:
                        scalar.copy(b_ext[:, 0:1], b_ext[:, SC:SC + 1])
                    scalar.wait_ge(s_bmm, s + 1)
                    scalar.copy(b_ext[:, 1:SC + 1], pb_b[:]).then_inc(s_bext, 1)
                    scalar.wait_ge(s_aTT, s + 1)
                    for m in range(NCC):
                        scalar.wait_ge(s_kvrmm, 24 * s + 16 + m + 1)
                        scalar.activation(ekri[:, m, :], pb_kvr[m % 2][:],
                                          AF.Sigmoid).then_inc(s_kvrev, 1)
                    for tb in range(TB):
                        tau = s * TB + tb
                        scalar.wait_ge(s_omm, tau + 1)
                        scalar.activation(dump[:, 0:512], pb_olo[tb % 2][:],
                                          AF.Copy,
                                          accum_out=st2[:, 4, tb:tb + 1])
                        scalar.activation(dump[:, 0:512], pb_ohi[tb % 2][:],
                                          AF.Copy,
                                          accum_out=st2[:, 5, tb:tb + 1])
                        scalar.activation(dump[:, 0:512], pb_olo[tb % 2][:],
                                          AF.Square,
                                          accum_out=st2[:, 2, tb:tb + 1])
                        scalar.activation(dump[:, 0:512], pb_ohi[tb % 2][:],
                                          AF.Square,
                                          accum_out=st2[:, 3, tb:tb + 1]
                                          ).then_inc(s_ostat, 1)
                        scalar.wait_ge(s_ch2, tau + 1)
                        scalar.activation(st2[:, 2, tb:tb + 1],
                                          st2[:, 2, tb:tb + 1], AF.Ln)
                        scalar.activation(st2[:, 3, tb:tb + 1],
                                          st2[:, 2, tb:tb + 1], AF.Exp,
                                          scale=-0.5).then_inc(s_le2, 1)
                    for tb in range(TB):
                        tau = s * TB + tb
                        scalar.wait_ge(s_xp, tau + 1)
                        scalar.activation(dump[:], xp_v[:, tb, 0:512],
                                          AF.Square,
                                          accum_out=st3[:, 2, tb:tb + 1])
                        scalar.activation(dump[:], xp_v[:, tb, 512:1024],
                                          AF.Square,
                                          accum_out=st3[:, 3, tb:tb + 1]
                                          ).then_inc(s_sq3, 1)
                    scalar.wait_ge(s_ch3, s + 1)
                    scalar.activation(st3[:, 2, :], st3[:, 2, :], AF.Ln)
                    scalar.activation(st3[:, 3, :], st3[:, 2, :], AF.Exp,
                                      scale=-0.5).then_inc(s_le3, 1)
                    scalar.wait_ge(s_nb3, s + 1)
                    for tb in range(TB):
                        prev = tm_reader.get(tb % 2)
                        if prev is not None:
                            psem, pcnt = prev
                            scalar.wait_ge(psem, pcnt)
                        tm_reader[tb % 2] = (s_x3f, x3f_done[(s, tb)])
                        scalar.activation(tm_stage[:, tb % 2, :],
                                          xp_v[:, tb, :], AF.Identity,
                                          bias=st3[:, 1, tb:tb + 1],
                                          scale=st3[:, 3, tb:tb + 1]
                                          ).then_inc(s_l3ap, 1)
                    for tb in range(TB):
                        tau = s * TB + tb
                        scalar.wait_ge(s_fmm, tau + 1)
                        scalar.activation(dump[:, 0:512], pb_olo[tb % 2][:],
                                          AF.Copy,
                                          accum_out=st4[:, 4, tb:tb + 1])
                        scalar.activation(dump[:, 0:512], pb_ohi[tb % 2][:],
                                          AF.Copy,
                                          accum_out=st4[:, 5, tb:tb + 1])
                        scalar.activation(dump[:, 0:512], pb_olo[tb % 2][:],
                                          AF.Square,
                                          accum_out=st4[:, 2, tb:tb + 1])
                        scalar.activation(dump[:, 0:512], pb_ohi[tb % 2][:],
                                          AF.Square,
                                          accum_out=st4[:, 3, tb:tb + 1]
                                          ).then_inc(s_fstat, 1)
                        scalar.wait_ge(s_ch4, tau + 1)
                        scalar.activation(st4[:, 2, tb:tb + 1],
                                          st4[:, 2, tb:tb + 1], AF.Ln)
                        scalar.activation(st4[:, 3, tb:tb + 1],
                                          st4[:, 2, tb:tb + 1], AF.Exp,
                                          scale=-0.5).then_inc(s_le4, 1)

        @block.vector
        def _(vector):
            vector.memset(consts[:, 0:1], -0.5)
            vector.memset(consts[:, 1:2], -1.0).then_inc(s_cmem, 1)
            vector.memset(carry[:], 0.0)
            vector.memset(dcar[:], 0.0)
            vector.memset(a_ext[:, :, 0:1], 0.0)
            vector.memset(ln1_fm[:, :, LP - 1:LP], 0.0)
            vector.memset(b_ext[:, 0:1], 0.0)
            vector.wait_ge(s_cst, 160)
            vector.wait_ge(s_lam, 1)
            vector.tensor_tensor(lrhs[:], lam16[:].broadcast_to([H, NCC]),
                                 hm8[:], ALU.mult).then_inc(s_lch, 1)
            vector.wait_ge(s_lam, 2)
            for it in range(-1, nsc):
                sf = it + 1
                if sf < nsc:
                    if sf >= 1:
                        vector.tensor_copy(ln1_fm[:, :, LP - 1:LP],
                                           ln1_fm[:, :, SC + LP - 1:SC + LP])
                    vector.sem_inc(s_l1b, 1)
                    for tb in range(TB):
                        q = sf * TB + tb
                        vector.wait_ge(s_slot[q % XR], xq_ready[q])
                        vector.reduce_sum(st1[:, 4, tb:tb + 1],
                                          x_tm[:, q % XR, :],
                                          mybir.AxisListType.X)
                    vector.wait_ge(s_sq1, TB * (sf + 1))
                    vector.tensor_tensor(st1[:, 5, :], st1[:, 2, :],
                                         st1[:, 3, :], ALU.add)
                    vector.tensor_scalar(st1[:, 0, :], st1[:, 4, :],
                                         1.0 / C, None, ALU.mult)
                    vector.tensor_scalar(st1[:, 2, :], st1[:, 5, :],
                                         1.0 / C, EPS, ALU.mult, ALU.add)
                    vsync(vector, vector.tensor_tensor(
                        st1[:, 1, :], st1[:, 0, :], st1[:, 0, :], ALU.mult))
                    vector.tensor_tensor(st1[:, 2, :], st1[:, 2, :],
                                         st1[:, 1, :], ALU.subtract
                                         ).then_inc(s_ch1, 1)
                    vector.wait_ge(s_le1, sf + 1)
                    vector.scalar_tensor_tensor(st1[:, 1, :], st1[:, 0, :],
                                                -1.0, st1[:, 3, :],
                                                ALU.mult, ALU.mult
                                                ).then_inc(s_nb1, 1)
                    if it == -1:
                        vector.wait_ge(s_x1f, 16 * 32 * (sf + 1))
                        vector.tensor_tensor(diff[:],
                                             ln1_fm[:, :, LP:SC + LP],
                                             ln1_fm[:, :, LP - 1:SC + LP - 1],
                                             ALU.subtract).then_inc(s_diff, 1)
                        wseq(vector, s_kvrmm, 24 * sf)
                        for pi, xm in enumerate((xmk, xmv, diff)):
                            for ic in range(NCC):
                                vector.tensor_scalar(
                                    xm[:, ic, :], diff[:, ic, :],
                                    ((mkc, mvc, mrc)[pi])[:, ic:ic + 1],
                                    None, ALU.mult)
                            vector.tensor_tensor(
                                xm[:], xm[:],
                                ln1_fm[:, :, LP - 1:SC + LP - 1], ALU.add
                            ).then_inc(s_mix, 1)
                    if it == -1:
                        for tb in range(TB):
                            q = tb
                            vector.wait_ge(s_slot[q % XR], xq_ready[q])
                            vector.tensor_copy(xres[:, tb, :],
                                               x_tm[:, q % XR, :]
                                               ).then_inc(s_xcp, 1)
                if it >= 0:
                    s = it
                    if s >= 1:
                        vector.tensor_copy(a_ext[:, :, 0:1],
                                           a_ext[:, :, SC:SC + 1])
                    vector.wait_ge(s_kvrev, 24 * s + 16)
                    vector.tensor_tensor(a_ext[:, :, 1:SC + 1], ekri[:],
                                         vvnum[:], ALU.mult
                                         ).then_inc(s_aTT, 1)
                    wseq(vector, s_l3ap, TB * s)
                    for m in range(NCC):
                        sc_i = vector.tensor_tensor_scan(
                            scan_v[:, m, :],
                            lam_ch[:, m:m + 1].broadcast_to([128, SC]),
                            a_ext[:, m, 0:SC], carry[:, m:m + 1],
                            ALU.mult, ALU.add)
                    vsync(vector, sc_i)
                    vector.tensor_copy(carry[:], scan_v[:, :, SC - 1])
                    vector.wait_ge(s_bext, s + 1)
                    dsc_i = vector.tensor_tensor_scan(
                        den_so, lam16[:].broadcast_to([H, SC]),
                        b_ext[:, 0:SC], dcar[:], ALU.mult, ALU.add)
                    vsync(vector, dsc_i)
                    vector.tensor_copy(dcar[:], den_so[:, SC - 1:SC])
                    vector.tensor_scalar(b_ext[:, 0:SC], b_ext[:, 1:SC + 1],
                                         eu16[:], 1e-6, ALU.mult, ALU.max)
                    vector.tensor_tensor(den_so, den_so,
                                         b_ext[:, 0:SC], ALU.add)
                    vector.reciprocal(b_ext[:, 0:SC], den_so)
                    vector.tensor_copy(den_so.bitcast(BF16)[:, 0:SC],
                                       b_ext[:, 0:SC]).then_inc(s_den, 1)
                    for m in range(NCC):
                        vector.scalar_tensor_tensor(
                            vvnum[:, m, :], a_ext[:, m, 1:SC + 1],
                            eu_ch[:, m:m + 1], scan_v[:, m, :],
                            ALU.mult, ALU.add)
                    for m in range(NCC):
                        vector.wait_ge(s_kvrev, 24 * s + 16 + m + 1)
                        vector.wait_ge(s_rmm, 8 * s + m + 1)
                        vector.tensor_tensor(ekri[:, m, :], ekri[:, m, :],
                                             pb_r[:], ALU.mult
                                             ).then_inc(s_rinv, 1)
                    wseq(vector, s_fmm, TB * s)
                    vector.tensor_tensor(rwkv[:], vvnum[:], ekri[:], ALU.mult
                                         ).then_inc(s_rwkv, 1)
                    if s + 1 < nsc:
                        vector.wait_ge(s_x1f, 16 * 32 * (s + 2))
                        vector.tensor_tensor(diff[:],
                                             ln1_fm[:, :, LP:SC + LP],
                                             ln1_fm[:, :, LP - 1:SC + LP - 1],
                                             ALU.subtract).then_inc(s_diff, 1)
                        wseq(vector, s_kvrmm, 24 * (s + 1))
                        for pi, xm in enumerate((xmk, xmv, diff)):
                            for ic in range(NCC):
                                vector.tensor_scalar(
                                    xm[:, ic, :], diff[:, ic, :],
                                    ((mkc, mvc, mrc)[pi])[:, ic:ic + 1],
                                    None, ALU.mult)
                            vector.tensor_tensor(
                                xm[:], xm[:],
                                ln1_fm[:, :, LP - 1:SC + LP - 1], ALU.add
                            ).then_inc(s_mix, 1)
                    for tb in range(TB):
                        tau = s * TB + tb
                        q = tau
                        vector.wait_ge(s_ostat, tau + 1)
                        vector.tensor_tensor(st2[:, 4, tb:tb + 1],
                                             st2[:, 4, tb:tb + 1],
                                             st2[:, 5, tb:tb + 1], ALU.add)
                        vector.tensor_tensor(st2[:, 5, tb:tb + 1],
                                             st2[:, 2, tb:tb + 1],
                                             st2[:, 3, tb:tb + 1], ALU.add)
                        vector.tensor_scalar(st2[:, 0, tb:tb + 1],
                                             st2[:, 4, tb:tb + 1],
                                             1.0 / C, None, ALU.mult)
                        vector.tensor_scalar(st2[:, 2, tb:tb + 1],
                                             st2[:, 5, tb:tb + 1],
                                             1.0 / C, EPS, ALU.mult, ALU.add)
                        vsync(vector, vector.tensor_tensor(
                            st2[:, 1, tb:tb + 1], st2[:, 0, tb:tb + 1],
                            st2[:, 0, tb:tb + 1], ALU.mult))
                        vector.tensor_tensor(st2[:, 2, tb:tb + 1],
                                             st2[:, 2, tb:tb + 1],
                                             st2[:, 1, tb:tb + 1],
                                             ALU.subtract).then_inc(s_ch2, 1)
                        vector.wait_ge(s_le2, tau + 1)
                        vector.scalar_tensor_tensor(st2[:, 1, tb:tb + 1],
                                                    st2[:, 0, tb:tb + 1],
                                                    -1.0,
                                                    st2[:, 3, tb:tb + 1],
                                                    ALU.mult, ALU.mult
                                                    ).then_inc(s_nb2, 1)
                        vector.wait_ge(s_nb2, tau + 1)
                        vector.tensor_scalar(xres[:, tb, :],
                                             xres[:, tb, :],
                                             st2[:, 1, tb:tb + 1], None,
                                             ALU.add)
                        vector.scalar_tensor_tensor(
                            xp_v[:, tb, 0:512], pb_olo[tb % 2][:],
                            st2[:, 3, tb:tb + 1], xres[:, tb, 0:512],
                            ALU.mult, ALU.add)
                        vector.scalar_tensor_tensor(
                            xp_v[:, tb, 512:1024], pb_ohi[tb % 2][:],
                            st2[:, 3, tb:tb + 1], xres[:, tb, 512:1024],
                            ALU.mult, ALU.add).then_inc(s_xp, 1)
                        vector.reduce_sum(st3[:, 4, tb:tb + 1],
                                          xp_v[:, tb, :],
                                          mybir.AxisListType.X)
                    vector.wait_ge(s_sq3, TB * (s + 1))
                    vector.tensor_tensor(st3[:, 5, :], st3[:, 2, :],
                                         st3[:, 3, :], ALU.add)
                    vector.tensor_scalar(st3[:, 0, :], st3[:, 4, :],
                                         1.0 / C, None, ALU.mult)
                    vector.tensor_scalar(st3[:, 2, :], st3[:, 5, :],
                                         1.0 / C, EPS, ALU.mult, ALU.add)
                    vsync(vector, vector.tensor_tensor(
                        st3[:, 1, :], st3[:, 0, :], st3[:, 0, :], ALU.mult))
                    vector.tensor_tensor(st3[:, 2, :], st3[:, 2, :],
                                         st3[:, 1, :], ALU.subtract
                                         ).then_inc(s_ch3, 1)
                    vector.wait_ge(s_le3, s + 1)
                    vector.scalar_tensor_tensor(st3[:, 1, :], st3[:, 0, :],
                                                -1.0, st3[:, 3, :],
                                                ALU.mult, ALU.mult
                                                ).then_inc(s_nb3, 1)
                    if s + 1 < nsc:
                        for tb in range(TB):
                            q = (s + 1) * TB + tb
                            vector.wait_ge(s_slot[q % XR], xq_ready[q])
                            vector.tensor_copy(xres[:, tb, :],
                                               x_tm[:, q % XR, :]
                                               ).then_inc(s_xcp, 1)
                    for tb in range(TB):
                        tau = s * TB + tb
                        vector.wait_ge(s_fstat, tau + 1)
                        vector.tensor_tensor(st4[:, 4, tb:tb + 1],
                                             st4[:, 4, tb:tb + 1],
                                             st4[:, 5, tb:tb + 1], ALU.add)
                        vector.tensor_tensor(st4[:, 5, tb:tb + 1],
                                             st4[:, 2, tb:tb + 1],
                                             st4[:, 3, tb:tb + 1], ALU.add)
                        vector.tensor_scalar(st4[:, 0, tb:tb + 1],
                                             st4[:, 4, tb:tb + 1],
                                             1.0 / C, None, ALU.mult)
                        vector.tensor_scalar(st4[:, 2, tb:tb + 1],
                                             st4[:, 5, tb:tb + 1],
                                             1.0 / C, EPS, ALU.mult, ALU.add)
                        vsync(vector, vector.tensor_tensor(
                            st4[:, 1, tb:tb + 1], st4[:, 0, tb:tb + 1],
                            st4[:, 0, tb:tb + 1], ALU.mult))
                        vector.tensor_tensor(st4[:, 2, tb:tb + 1],
                                             st4[:, 2, tb:tb + 1],
                                             st4[:, 1, tb:tb + 1],
                                             ALU.subtract).then_inc(s_ch4, 1)
                        vector.wait_ge(s_le4, tau + 1)
                        vector.scalar_tensor_tensor(st4[:, 1, tb:tb + 1],
                                                    st4[:, 0, tb:tb + 1],
                                                    -1.0,
                                                    st4[:, 3, tb:tb + 1],
                                                    ALU.mult, ALU.mult
                                                    ).then_inc(s_nb4, 1)
                        vector.wait_ge(s_nb4, tau + 1)
                        vector.wait_ge(s_l3ap, tau + 1)
                        vector.tensor_scalar(xp_v[:, tb, :], xp_v[:, tb, :],
                                             st4[:, 1, tb:tb + 1], None,
                                             ALU.add)
                        wseq(vector, s_od, 16 * tau)
                        vector.scalar_tensor_tensor(
                            out_r[:, 0:512], pb_olo[tb % 2][:],
                            st4[:, 3, tb:tb + 1], xp_v[:, tb, 0:512],
                            ALU.mult, ALU.add)
                        vector.scalar_tensor_tensor(
                            out_r[:, 512:1024], pb_ohi[tb % 2][:],
                            st4[:, 3, tb:tb + 1], xp_v[:, tb, 512:1024],
                            ALU.mult, ALU.add).then_inc(s_out, 1)
    return nc


def kernel(L0_attn_x, L0_attn_mix_k, L0_attn_mix_v, L0_attn_mix_r,
           L0_attn_Wk, L0_attn_Wv, L0_attn_Wr, L0_attn_Wo,
           L0_attn_time_decay, L0_attn_time_first, L0_ffn_weight):
    B, T, _ = L0_attn_x.shape
    nsc = T // SC
    x = np.ascontiguousarray(np.asarray(L0_attn_x, dtype=np.float32))

    def wT(w):
        return np.ascontiguousarray(np.asarray(w, dtype=np.float32).T)

    def cols(v):
        return np.ascontiguousarray(
            np.asarray(v, dtype=np.float32).reshape(NCC, 128).T)

    tf = np.asarray(L0_attn_time_first, dtype=np.float32)
    td = np.asarray(L0_attn_time_decay, dtype=np.float32)
    u_cols = cols(np.repeat(tf, C // H))

    p_ge64 = (np.arange(128) >= 64).astype(np.int64)
    parity = ((np.arange(H)[:, None] % 2) == p_ge64[None, :]
              ).astype(np.float32)
    hm8 = (np.arange(H)[:, None] // 2 == np.arange(NCC)[None, :]
           ).astype(np.float32)
    head_of = 2 * np.arange(NCC)[:, None, None] + p_ge64[None, :, None]
    hm16 = (np.arange(H)[None, None, :] == head_of).astype(bf16)
    sel2 = np.ascontiguousarray(np.transpose(hm16, (0, 2, 1)))

    shared = {
        "wkT": wT(L0_attn_Wk), "wvT": wT(L0_attn_Wv), "wrT": wT(L0_attn_Wr),
        "woT": wT(L0_attn_Wo), "wfT": wT(L0_ffn_weight),
        "mkc": cols(np.asarray(L0_attn_mix_k).reshape(-1)),
        "mvc": cols(np.asarray(L0_attn_mix_v).reshape(-1)),
        "mrc": cols(np.asarray(L0_attn_mix_r).reshape(-1)),
        "ucols": u_cols,
        "tdcol": np.ascontiguousarray(td.reshape(H, 1)),
        "tfcol": np.ascontiguousarray(tf.reshape(H, 1)),
        "parity": parity, "hm8": hm8,
        "hm16": np.ascontiguousarray(hm16),
        "sel2": sel2,
    }
    nc = build_program(nsc)
    in_maps = [dict(shared, x=np.ascontiguousarray(x[b])) for b in range(B)]
    res = run_bass_kernel_spmd(nc, in_maps, core_ids=list(range(B)))
    return np.stack([np.asarray(r["out"], dtype=np.float32)
                     for r in res.results], axis=0)


# version 04
# speedup vs baseline: 2.1940x; 1.0115x over previous
import numpy as np
import ml_dtypes
import concourse.bass as bass
import concourse.mybir as mybir
from concourse.bass_utils import run_bass_kernel_spmd
from contextlib import ExitStack

F32 = mybir.dt.float32
BF16 = mybir.dt.bfloat16
AF = mybir.ActivationFunctionType
ALU = mybir.AluOpType
bf16 = ml_dtypes.bfloat16

DBG = []

C = 1024
H = 16
NCC = 8
SC = 512
TB = 4
LP = 16
XR = 5
EPS = 1e-5


def build_program(nsc):
    T = nsc * SC
    NQ = nsc * TB
    nc = bass.Bass("TRN2", target_bir_lowering=False, debug=False,
                   detect_race_conditions=False)

    x_h = nc.dram_tensor("x", [T, C], F32, kind="ExternalInput").ap()
    w_h = {}
    for nm in ("wkT", "wvT", "wrT", "woT", "wfT"):
        w_h[nm] = nc.dram_tensor(nm, [C, C], F32, kind="ExternalInput").ap()
    mk_h = nc.dram_tensor("mkc", [128, NCC], F32, kind="ExternalInput").ap()
    mv_h = nc.dram_tensor("mvc", [128, NCC], F32, kind="ExternalInput").ap()
    mr_h = nc.dram_tensor("mrc", [128, NCC], F32, kind="ExternalInput").ap()
    uc_h = nc.dram_tensor("ucols", [128, NCC], F32, kind="ExternalInput").ap()
    td_h = nc.dram_tensor("tdcol", [H, 1], F32, kind="ExternalInput").ap()
    tf_h = nc.dram_tensor("tfcol", [H, 1], F32, kind="ExternalInput").ap()
    par_h = nc.dram_tensor("parity", [H, 128], F32, kind="ExternalInput").ap()
    hm8_h = nc.dram_tensor("hm8", [H, NCC], F32, kind="ExternalInput").ap()
    hm16_h = nc.dram_tensor("hm16", [NCC, 128, H], BF16, kind="ExternalInput").ap()
    sel2_h = nc.dram_tensor("sel2", [NCC, H, 128], BF16, kind="ExternalInput").ap()
    out_h = nc.dram_tensor("out", [T, C], F32, kind="ExternalOutput").ap()

    es = ExitStack()

    def sb(name, shape, dt):
        return es.enter_context(nc.sbuf_tensor(name, shape, dt))

    def ps(name, shape):
        return es.enter_context(nc.psum_tensor(name, shape, F32))

    def sem(name):
        return es.enter_context(nc.semaphore(name))

    with es:
        wk = sb("wk", [128, NCC, C], BF16)
        wv = sb("wv", [128, NCC, C], BF16)
        wr = sb("wr", [128, NCC, C], BF16)
        wo = sb("wo", [128, NCC, C], BF16)
        wf = sb("wf", [128, NCC, C], BF16)
        x_tm = sb("x_tm", [128, XR, C], F32)
        tm_stage = sb("tm_stage", [128, 2, C], BF16)
        ln1_fm = sb("ln1_fm", [128, NCC, SC + LP], BF16)
        diff = sb("diff", [128, NCC, SC], BF16)
        xmk = sb("xmk", [128, NCC, SC], BF16)
        xmv = sb("xmv", [128, NCC, SC], BF16)
        ekri = sb("ekri", [128, NCC, SC], BF16)
        vvnum = sb("vvnum", [128, NCC, SC], BF16)
        xres = sb("xres", [128, TB, C], BF16)
        a_ext = sb("a_ext", [128, NCC, SC + 1], BF16)
        big0 = sb("big0", [128, TB * C], F32)
        rwkv = sb("rwkv", [128, NCC, SC], BF16)
        out_r = sb("out_r", [128, C], F32)
        dump = sb("dump", [128, 512], BF16)
        lam_ch = sb("lam_ch", [128, NCC], F32)
        eu_ch = sb("eu_ch", [128, NCC], F32)
        mkc = sb("mkc_s", [128, NCC], F32)
        mvc = sb("mvc_s", [128, NCC], F32)
        mrc = sb("mrc_s", [128, NCC], F32)
        ucs = sb("ucs", [128, NCC], F32)
        carry = sb("carry", [128, NCC], F32)
        tdc = sb("tdc", [H, 1], F32)
        tfc = sb("tfc", [H, 1], F32)
        e16 = sb("e16", [H, 1], F32)
        lam16 = sb("lam16", [H, 1], F32)
        eu16 = sb("eu16", [H, 1], F32)
        lrhs = sb("lrhs", [H, NCC], F32)
        hm8 = sb("hm8_s", [H, NCC], F32)
        hm16 = sb("hm16_s", [128, NCC, H], BF16)
        sel2 = sb("sel2_s", [H, NCC, 128], BF16)
        b_ext_t = sb("b_ext", [H, SC + 1], F32)
        den_so_t = sb("den_so", [H, SC], F32)
        b_ext = b_ext_t[:]
        den_so = den_so_t[:]
        dcar = sb("dcar", [H, 1], F32)
        consts = sb("consts", [128, 2], F32)
        st1 = sb("st1", [128, 6, TB], F32)
        st2 = sb("st2", [128, 6, TB], F32)
        st3 = sb("st3", [128, 6, TB], F32)
        st4 = sb("st4", [128, 6, TB], F32)

        tens = dict(wk=wk, wv=wv, wr=wr, wo=wo, wf=wf, x_tm=x_tm,
                    tm_stage=tm_stage, ln1_fm=ln1_fm, diff=diff, xmk=xmk,
                    xmv=xmv, ekri=ekri, vvnum=vvnum, xres=xres, a_ext=a_ext,
                    big0=big0, rwkv=rwkv, out_r=out_r, dump=dump,
                    lam_ch=lam_ch, eu_ch=eu_ch, carry=carry, lam16=lam16,
                    eu16=eu16, b_ext=b_ext_t, den_so=den_so_t, dcar=dcar,
                    st1=st1, st2=st2, st3=st3, st4=st4, hm16=hm16, sel2=sel2)
        dbg_out = {}
        for nm in DBG:
            t = tens[nm]
            dbg_out[nm] = nc.dram_tensor(
                "dbg_" + nm, list(t[:].shape), t[:].dtype,
                kind="ExternalOutput").ap()

        pb_kvr = [ps("pb_kvr0", [128, 512]), ps("pb_kvr1", [128, 512])]
        pb_olo = [ps("pb_olo0", [128, 512]), ps("pb_olo1", [128, 512])]
        pb_ohi = [ps("pb_ohi0", [128, 512]), ps("pb_ohi1", [128, 512])]
        pb_b = ps("pb_b", [H, 512])
        pb_r = ps("pb_r", [128, 512])

        s_cst = sem("s_cst")
        s_wc = [sem(f"s_wc{i}") for i in range(4)]
        s_slot = [sem(f"s_slot{j}") for j in range(XR)]
        slot_exp = [0] * XR
        xq_ready = {}
        outd_done = {}
        wf_done = {}
        x1f_cnt = [0]
        x3f_cnt = [0]
        x1f_done = {}
        x3f_done = {}
        tm_reader = {}
        s_wfsg = sem("s_wfsg")
        s_cmem = sem("s_cmem")
        s_lam = sem("s_lam")
        s_lch = sem("s_lch")
        s_l1b = sem("s_l1b")
        s_l1ap = sem("s_l1ap")
        s_x1f = sem("s_x1f")
        s_diff = sem("s_diff")
        s_sq1 = sem("s_sq1")
        s_ch1 = sem("s_ch1")
        s_le1 = sem("s_le1")
        s_nb1 = sem("s_nb1")
        s_mixp = sem("s_mixp")
        s_mix = sem("s_mix")
        s_kvrmm = sem("s_kvrmm")
        s_kvrev = sem("s_kvrev")
        s_bmm = sem("s_bmm")
        s_bext = sem("s_bext")
        s_den = sem("s_den")
        s_rmm = sem("s_rmm")
        s_rinv = sem("s_rinv")
        s_rwkv = sem("s_rwkv")
        s_aTT = sem("s_aTT")
        s_vx = sem("s_vx")
        s_xcp = sem("s_xcp")
        s_omm = sem("s_omm")
        s_ostat = sem("s_ostat")
        s_ch2 = sem("s_ch2")
        s_le2 = sem("s_le2")
        s_nb2 = sem("s_nb2")
        s_xnb = sem("s_xnb")
        s_xp = sem("s_xp")
        s_sq3 = sem("s_sq3")
        s_ch3 = sem("s_ch3")
        s_le3 = sem("s_le3")
        s_nb3 = sem("s_nb3")
        s_l3ap = sem("s_l3ap")
        s_x3f = sem("s_x3f")
        s_fmm = sem("s_fmm")
        s_fstat = sem("s_fstat")
        s_ch4 = sem("s_ch4")
        s_le4 = sem("s_le4")
        s_nb4 = sem("s_nb4")
        s_xnb4 = sem("s_xnb4")
        s_out = sem("s_out")
        s_od = sem("s_od")

        nc.const_aps.register(F32, -0.5, consts[:, 0:1])
        nc.const_aps.register(F32, -1.0, consts[:, 1:2])

        block = es.enter_context(nc.Block())

        vx_n = [0]

        def vsync(vector, inst):
            vx_n[0] += 1
            inst.then_inc(s_vx, 1)
            vector.wait_ge(s_vx, vx_n[0])

        scan_v = big0[:].rearrange("p (a b) -> p a b", a=NCC)
        xp_v = big0[:].rearrange("p (a b) -> p a b", a=TB)

        def wseq(eng, semh, val):
            if val > 0:
                eng.wait_ge(semh, val)

        pair_usages = []

        def pair_wait(tensor, kind, s, tb):
            pair_usages.append((kind, s, tb))
            g = len(pair_usages) - 1
            if g >= 2:
                k2, s2, tb2 = pair_usages[g - 2]
                tau2 = s2 * TB + tb2
                if k2 == "o":
                    tensor.wait_ge(s_ostat, tau2 + 1)
                    tensor.wait_ge(s_xp, tau2 + 1)
                else:
                    tensor.wait_ge(s_fstat, tau2 + 1)
                    tensor.wait_ge(s_out, tau2 + 1)

        @block.sync
        def _(sync):
            for ap_h, t in ((mk_h, mkc), (mv_h, mvc), (mr_h, mrc),
                            (uc_h, ucs), (td_h, tdc), (tf_h, tfc),
                            (hm8_h, hm8)):
                sync.dma_start(out=t[:], in_=ap_h).then_inc(s_cst, 16)
            sync.dma_start(out=big0[0:16, 0:128], in_=par_h
                           ).then_inc(s_cst, 16)
            sync.dma_start(out=hm16[:], in_=hm16_h.transpose([1, 0, 2])
                           ).then_inc(s_cst, 16)
            sync.dma_start(out=sel2[:], in_=sel2_h.transpose([1, 0, 2])
                           ).then_inc(s_cst, 16)
            for ic in range(NCC):
                if ic >= 2:
                    sync.wait_ge(s_wfsg, ic - 1)
                j = ic % 2
                slot_exp[j] += 16
                wf_done[ic] = slot_exp[j]
                sync.dma_start(out=x_tm[:, j, :],
                               in_=w_h["wfT"][ic * 128:(ic + 1) * 128, :]
                               ).then_inc(s_slot[j], 16)
            sync.wait_ge(s_wfsg, NCC)
            for it in range(-1, nsc):
                sf = it + 1
                if sf < nsc:
                    for tb in range(TB):
                        q = sf * TB + tb
                        j = q % XR
                        wseq(sync, s_l1ap, q - XR + 1)
                        wseq(sync, s_xcp, q - XR + 1)
                        if q - XR >= 0:
                            sync.wait_ge(s_nb1, (q - XR) // TB + 1)
                        slot_exp[j] += 16
                        xq_ready[q] = slot_exp[j]
                        sync.dma_start(out=x_tm[:, j, :],
                                       in_=x_h[q * 128:(q + 1) * 128, :]
                                       ).then_inc(s_slot[j], 16)
                    sync.wait_ge(s_l1b, sf + 1)
                    wseq(sync, s_mix, 3 * sf)
                    for tb in range(TB):
                        sync.wait_ge(s_l1ap, sf * TB + tb + 1)
                        for ic in range(NCC):
                            sync.dma_start_transpose(
                                ln1_fm[:, ic,
                                       LP + tb * 128:LP + (tb + 1) * 128],
                                tm_stage[:, tb % 2, ic * 128:(ic + 1) * 128]
                            ).then_inc(s_x1f, 16)
                            x1f_cnt[0] += 16
                        x1f_done[(sf, tb)] = x1f_cnt[0]
                if it >= 0:
                    s = it
                    for tb in range(TB):
                        q = s * TB + tb
                        sync.wait_ge(s_out, q + 1)
                        sync.dma_start(out=out_h[q * 128:(q + 1) * 128, :],
                                       in_=out_r[:]
                                       ).then_inc(s_od, 16)
            sync.wait_ge(s_od, 16 * NQ)
            ndbg = 0
            for nm in DBG:
                sync.dma_start(out=dbg_out[nm], in_=tens[nm][:]
                               ).then_inc(s_cst, 16)
                ndbg += 16
            if ndbg:
                sync.wait_ge(s_cst, 160 + ndbg)

        @block.gpsimd
        def _(g):
            for wi, (nm, t) in enumerate((("wkT", wk), ("wvT", wv),
                                          ("wrT", wr), ("woT", wo))):
                wsrc = w_h[nm].rearrange("(a p) o -> p a o", p=128)
                g.dma_start(out=t[:], in_=wsrc).then_inc(s_wc[wi], 16)
            g.wait_ge(s_cst, 160)

        @block.tensor
        def _(tensor):
            tensor.wait_ge(s_cst, 160)
            tensor.wait_ge(s_lch, 1)
            tensor.matmul(pb_kvr[0][:, 0:NCC], big0[0:16, 0:128], lrhs[:],
                          start=True, stop=True).then_inc(s_lch, 1)
            tensor.wait_ge(s_lam, 2)
            kvr_fill = 0
            for s in range(nsc):
                for pi in range(3):
                    if pi == 2:
                        tensor.wait_ge(s_kvrev, 24 * s + 8)
                        wseq(tensor, s_bext, s)
                        for m in range(NCC):
                            mm = tensor.matmul(pb_b[:], hm16[:, m, :],
                                               ekri[:, m, :],
                                               start=(m == 0),
                                               stop=(m == NCC - 1))
                            if m == NCC - 1:
                                mm.then_inc(s_bmm, 1)
                    w_t = (wk, wv, wr)[pi]
                    xm_t = (xmk, xmv, diff)[pi]
                    if s == 0:
                        tensor.wait_ge(s_wc[pi], 16)
                    tensor.wait_ge(s_mix, 3 * s + pi + 1)
                    for m in range(NCC):
                        wseq(tensor, s_kvrev, kvr_fill - 1)
                        for kc in range(NCC):
                            mm = tensor.matmul(
                                pb_kvr[kvr_fill % 2][:],
                                w_t[:, kc, m * 128:(m + 1) * 128],
                                xm_t[:, kc, :],
                                start=(kc == 0), stop=(kc == NCC - 1))
                            if kc == NCC - 1:
                                mm.then_inc(s_kvrmm, 1)
                        kvr_fill += 1
                tensor.wait_ge(s_den, s + 1)
                for m in range(NCC):
                    wseq(tensor, s_rinv, 8 * s + m)
                    tensor.matmul(pb_r[:], sel2[:, m, :],
                                  den_so.bitcast(BF16)[:, 0:SC],
                                  start=True, stop=True).then_inc(s_rmm, 1)
                tensor.wait_ge(s_rwkv, s + 1)
                if s == 0:
                    tensor.wait_ge(s_wc[3], 16)
                for tb in range(TB):
                    pair_wait(tensor, "o", s, tb)
                    for cc in range(NCC):
                        last = cc == NCC - 1
                        tensor.matmul(pb_olo[tb % 2][:],
                                      rwkv[:, cc, tb * 128:(tb + 1) * 128],
                                      wo[:, cc, 0:512],
                                      start=(cc == 0), stop=last)
                        mm = tensor.matmul(
                            pb_ohi[tb % 2][:],
                            rwkv[:, cc, tb * 128:(tb + 1) * 128],
                            wo[:, cc, 512:1024],
                            start=(cc == 0), stop=last)
                        if last:
                            mm.then_inc(s_omm, 1)
                if s == 0:
                    tensor.wait_ge(s_wfsg, NCC)
                for tb in range(TB):
                    tensor.wait_ge(s_x3f, 16 * 32 * (s + 1))
                    pair_wait(tensor, "f", s, tb)
                    for cc in range(NCC):
                        last = cc == NCC - 1
                        tensor.matmul(pb_olo[tb % 2][:],
                                      rwkv[:, cc, tb * 128:(tb + 1) * 128],
                                      wf[:, cc, 0:512],
                                      start=(cc == 0), stop=last)
                        mm = tensor.matmul(
                            pb_ohi[tb % 2][:],
                            rwkv[:, cc, tb * 128:(tb + 1) * 128],
                            wf[:, cc, 512:1024],
                            start=(cc == 0), stop=last)
                        if last:
                            mm.then_inc(s_fmm, 1)

        @block.scalar
        def _(scalar):
            for ic in range(NCC):
                scalar.wait_ge(s_slot[ic % 2], wf_done[ic])
                scalar.activation(wf[:, ic, :], x_tm[:, ic % 2, :], AF.Sign
                                  ).then_inc(s_wfsg, 1)
            scalar.wait_ge(s_cst, 160)
            scalar.wait_ge(s_cmem, 1)
            scalar.activation(e16[:], tdc[:], AF.Exp)
            scalar.activation(lam16[:], e16[:], AF.Exp, scale=-1.0
                              ).then_inc(s_lam, 1)
            scalar.activation(eu16[:], tfc[:], AF.Exp)
            scalar.activation(eu_ch[:], ucs[:], AF.Exp)
            scalar.wait_ge(s_lch, 2)
            scalar.copy(lam_ch[:], pb_kvr[0][:, 0:NCC]).then_inc(s_lam, 1)
            for it in range(-1, nsc):
                sf = it + 1
                if sf < nsc:
                    for tb in range(TB):
                        q = sf * TB + tb
                        scalar.wait_ge(s_slot[q % XR], xq_ready[q])
                        scalar.activation(dump[:], x_tm[:, q % XR, 0:512],
                                          AF.Square,
                                          accum_out=st1[:, 2, tb:tb + 1])
                        scalar.activation(dump[:], x_tm[:, q % XR, 512:1024],
                                          AF.Square,
                                          accum_out=st1[:, 3, tb:tb + 1]
                                          ).then_inc(s_sq1, 1)
                    scalar.wait_ge(s_ch1, sf + 1)
                    scalar.activation(st1[:, 2, :], st1[:, 2, :], AF.Ln)
                    scalar.activation(st1[:, 3, :], st1[:, 2, :], AF.Exp,
                                      scale=-0.5).then_inc(s_le1, 1)
                    scalar.wait_ge(s_nb1, sf + 1)
                    for tb in range(TB):
                        q = sf * TB + tb
                        prev = tm_reader.get(tb % 2)
                        if prev is not None:
                            psem, pcnt = prev
                            scalar.wait_ge(psem, pcnt)
                        tm_reader[tb % 2] = (s_x1f, x1f_done[(sf, tb)])
                        scalar.activation(tm_stage[:, tb % 2, :],
                                          x_tm[:, q % XR, :], AF.Identity,
                                          bias=st1[:, 1, tb:tb + 1],
                                          scale=st1[:, 3, tb:tb + 1]
                                          ).then_inc(s_l1ap, 1)
                if it >= 0:
                    s = it
                    for m in range(NCC):
                        scalar.wait_ge(s_kvrmm, 24 * s + m + 1)
                        scalar.activation(ekri[:, m, :], pb_kvr[m % 2][:],
                                          AF.Exp).then_inc(s_kvrev, 1)
                    for m in range(NCC):
                        scalar.wait_ge(s_kvrmm, 24 * s + 8 + m + 1)
                        scalar.copy(vvnum[:, m, :], pb_kvr[m % 2][:]
                                    ).then_inc(s_kvrev, 1)
                    if s >= 1:
                        scalar.copy(b_ext[:, 0:1], b_ext[:, SC:SC + 1])
                    scalar.wait_ge(s_bmm, s + 1)
                    scalar.copy(b_ext[:, 1:SC + 1], pb_b[:]).then_inc(s_bext, 1)
                    scalar.wait_ge(s_aTT, s + 1)
                    for m in range(NCC):
                        scalar.wait_ge(s_kvrmm, 24 * s + 16 + m + 1)
                        scalar.activation(ekri[:, m, :], pb_kvr[m % 2][:],
                                          AF.Sigmoid).then_inc(s_kvrev, 1)
                    for tb in range(TB):
                        tau = s * TB + tb
                        scalar.wait_ge(s_omm, tau + 1)
                        scalar.activation(dump[:, 0:512], pb_olo[tb % 2][:],
                                          AF.Copy,
                                          accum_out=st2[:, 4, tb:tb + 1])
                        scalar.activation(dump[:, 0:512], pb_ohi[tb % 2][:],
                                          AF.Copy,
                                          accum_out=st2[:, 5, tb:tb + 1])
                        scalar.activation(dump[:, 0:512], pb_olo[tb % 2][:],
                                          AF.Square,
                                          accum_out=st2[:, 2, tb:tb + 1])
                        scalar.activation(dump[:, 0:512], pb_ohi[tb % 2][:],
                                          AF.Square,
                                          accum_out=st2[:, 3, tb:tb + 1]
                                          ).then_inc(s_ostat, 1)
                        scalar.wait_ge(s_ch2, tau + 1)
                        scalar.activation(st2[:, 2, tb:tb + 1],
                                          st2[:, 2, tb:tb + 1], AF.Ln)
                        scalar.activation(st2[:, 3, tb:tb + 1],
                                          st2[:, 2, tb:tb + 1], AF.Exp,
                                          scale=-0.5).then_inc(s_le2, 1)
                    for tb in range(TB):
                        tau = s * TB + tb
                        scalar.wait_ge(s_xp, tau + 1)
                        scalar.activation(dump[:], xp_v[:, tb, 0:512],
                                          AF.Square,
                                          accum_out=st3[:, 2, tb:tb + 1])
                        scalar.activation(dump[:], xp_v[:, tb, 512:1024],
                                          AF.Square,
                                          accum_out=st3[:, 3, tb:tb + 1]
                                          ).then_inc(s_sq3, 1)
                    scalar.wait_ge(s_ch3, s + 1)
                    scalar.activation(st3[:, 2, :], st3[:, 2, :], AF.Ln)
                    scalar.activation(st3[:, 3, :], st3[:, 2, :], AF.Exp,
                                      scale=-0.5).then_inc(s_le3, 1)
                    scalar.wait_ge(s_nb3, s + 1)
                    for tb in range(TB):
                        prev = tm_reader.get(tb % 2)
                        if prev is not None:
                            psem, pcnt = prev
                            scalar.wait_ge(psem, pcnt)
                        scalar.activation(tm_stage[:, tb % 2, :],
                                          xp_v[:, tb, :], AF.Identity,
                                          bias=st3[:, 1, tb:tb + 1],
                                          scale=st3[:, 3, tb:tb + 1]
                                          ).then_inc(s_l3ap, 1)
                        if tb == 0:
                            scalar.wait_ge(s_omm, TB * (s + 1))
                        for ic in range(NCC):
                            scalar.dma_start_transpose(
                                rwkv[:, ic, tb * 128:(tb + 1) * 128],
                                tm_stage[:, tb % 2, ic * 128:(ic + 1) * 128]
                            ).then_inc(s_x3f, 16)
                            x3f_cnt[0] += 16
                        x3f_done[(s, tb)] = x3f_cnt[0]
                        tm_reader[tb % 2] = (s_x3f, x3f_done[(s, tb)])
                    for tb in range(TB):
                        tau = s * TB + tb
                        scalar.wait_ge(s_fmm, tau + 1)
                        scalar.activation(dump[:, 0:512], pb_olo[tb % 2][:],
                                          AF.Copy,
                                          accum_out=st4[:, 4, tb:tb + 1])
                        scalar.activation(dump[:, 0:512], pb_ohi[tb % 2][:],
                                          AF.Copy,
                                          accum_out=st4[:, 5, tb:tb + 1])
                        scalar.activation(dump[:, 0:512], pb_olo[tb % 2][:],
                                          AF.Square,
                                          accum_out=st4[:, 2, tb:tb + 1])
                        scalar.activation(dump[:, 0:512], pb_ohi[tb % 2][:],
                                          AF.Square,
                                          accum_out=st4[:, 3, tb:tb + 1]
                                          ).then_inc(s_fstat, 1)
                        scalar.wait_ge(s_ch4, tau + 1)
                        scalar.activation(st4[:, 2, tb:tb + 1],
                                          st4[:, 2, tb:tb + 1], AF.Ln)
                        scalar.activation(st4[:, 3, tb:tb + 1],
                                          st4[:, 2, tb:tb + 1], AF.Exp,
                                          scale=-0.5).then_inc(s_le4, 1)

        @block.vector
        def _(vector):
            vector.memset(consts[:, 0:1], -0.5)
            vector.memset(consts[:, 1:2], -1.0).then_inc(s_cmem, 1)
            vector.memset(carry[:], 0.0)
            vector.memset(dcar[:], 0.0)
            vector.memset(a_ext[:, :, 0:1], 0.0)
            vector.memset(ln1_fm[:, :, LP - 1:LP], 0.0)
            vector.memset(b_ext[:, 0:1], 0.0)
            vector.wait_ge(s_cst, 160)
            vector.wait_ge(s_lam, 1)
            vector.tensor_tensor(lrhs[:], lam16[:].broadcast_to([H, NCC]),
                                 hm8[:], ALU.mult).then_inc(s_lch, 1)
            vector.wait_ge(s_lam, 2)
            for it in range(-1, nsc):
                sf = it + 1
                if sf < nsc:
                    if sf >= 1:
                        vector.tensor_copy(ln1_fm[:, :, LP - 1:LP],
                                           ln1_fm[:, :, SC + LP - 1:SC + LP])
                    vector.sem_inc(s_l1b, 1)
                    for tb in range(TB):
                        q = sf * TB + tb
                        vector.wait_ge(s_slot[q % XR], xq_ready[q])
                        vector.reduce_sum(st1[:, 4, tb:tb + 1],
                                          x_tm[:, q % XR, :],
                                          mybir.AxisListType.X)
                    vector.wait_ge(s_sq1, TB * (sf + 1))
                    vector.tensor_tensor(st1[:, 5, :], st1[:, 2, :],
                                         st1[:, 3, :], ALU.add)
                    vector.tensor_scalar(st1[:, 0, :], st1[:, 4, :],
                                         1.0 / C, None, ALU.mult)
                    vector.tensor_scalar(st1[:, 2, :], st1[:, 5, :],
                                         1.0 / C, EPS, ALU.mult, ALU.add)
                    vsync(vector, vector.tensor_tensor(
                        st1[:, 1, :], st1[:, 0, :], st1[:, 0, :], ALU.mult))
                    vector.tensor_tensor(st1[:, 2, :], st1[:, 2, :],
                                         st1[:, 1, :], ALU.subtract
                                         ).then_inc(s_ch1, 1)
                    vector.wait_ge(s_le1, sf + 1)
                    vector.scalar_tensor_tensor(st1[:, 1, :], st1[:, 0, :],
                                                -1.0, st1[:, 3, :],
                                                ALU.mult, ALU.mult
                                                ).then_inc(s_nb1, 1)
                    if it == -1:
                        vector.wait_ge(s_x1f, 16 * 32 * (sf + 1))
                        vector.tensor_tensor(diff[:],
                                             ln1_fm[:, :, LP:SC + LP],
                                             ln1_fm[:, :, LP - 1:SC + LP - 1],
                                             ALU.subtract).then_inc(s_diff, 1)
                        wseq(vector, s_kvrmm, 24 * sf)
                        for pi, xm in enumerate((xmk, xmv, diff)):
                            for ic in range(NCC):
                                vector.tensor_scalar(
                                    xm[:, ic, :], diff[:, ic, :],
                                    ((mkc, mvc, mrc)[pi])[:, ic:ic + 1],
                                    None, ALU.mult)
                            vector.tensor_tensor(
                                xm[:], xm[:],
                                ln1_fm[:, :, LP - 1:SC + LP - 1], ALU.add
                            ).then_inc(s_mix, 1)
                    if it == -1:
                        for tb in range(TB):
                            q = tb
                            vector.wait_ge(s_slot[q % XR], xq_ready[q])
                            vector.tensor_copy(xres[:, tb, :],
                                               x_tm[:, q % XR, :]
                                               ).then_inc(s_xcp, 1)
                if it >= 0:
                    s = it
                    if s >= 1:
                        vector.tensor_copy(a_ext[:, :, 0:1],
                                           a_ext[:, :, SC:SC + 1])
                    vector.wait_ge(s_kvrev, 24 * s + 16)
                    vector.tensor_tensor(a_ext[:, :, 1:SC + 1], ekri[:],
                                         vvnum[:], ALU.mult
                                         ).then_inc(s_aTT, 1)
                    wseq(vector, s_l3ap, TB * s)
                    for m in range(NCC):
                        sc_i = vector.tensor_tensor_scan(
                            scan_v[:, m, :],
                            lam_ch[:, m:m + 1].broadcast_to([128, SC]),
                            a_ext[:, m, 0:SC], carry[:, m:m + 1],
                            ALU.mult, ALU.add)
                    vsync(vector, sc_i)
                    vector.tensor_copy(carry[:], scan_v[:, :, SC - 1])
                    vector.wait_ge(s_bext, s + 1)
                    dsc_i = vector.tensor_tensor_scan(
                        den_so, lam16[:].broadcast_to([H, SC]),
                        b_ext[:, 0:SC], dcar[:], ALU.mult, ALU.add)
                    vsync(vector, dsc_i)
                    vector.tensor_copy(dcar[:], den_so[:, SC - 1:SC])
                    vector.tensor_scalar(b_ext[:, 0:SC], b_ext[:, 1:SC + 1],
                                         eu16[:], 1e-6, ALU.mult, ALU.max)
                    vector.tensor_tensor(den_so, den_so,
                                         b_ext[:, 0:SC], ALU.add)
                    vector.reciprocal(b_ext[:, 0:SC], den_so)
                    vector.tensor_copy(den_so.bitcast(BF16)[:, 0:SC],
                                       b_ext[:, 0:SC]).then_inc(s_den, 1)
                    for m in range(NCC):
                        vector.scalar_tensor_tensor(
                            vvnum[:, m, :], a_ext[:, m, 1:SC + 1],
                            eu_ch[:, m:m + 1], scan_v[:, m, :],
                            ALU.mult, ALU.add)
                    for m in range(NCC):
                        vector.wait_ge(s_kvrev, 24 * s + 16 + m + 1)
                        vector.wait_ge(s_rmm, 8 * s + m + 1)
                        vector.tensor_tensor(ekri[:, m, :], ekri[:, m, :],
                                             pb_r[:], ALU.mult
                                             ).then_inc(s_rinv, 1)
                    wseq(vector, s_fmm, TB * s)
                    vector.tensor_tensor(rwkv[:], vvnum[:], ekri[:], ALU.mult
                                         ).then_inc(s_rwkv, 1)
                    if s + 1 < nsc:
                        vector.wait_ge(s_x1f, 16 * 32 * (s + 2))
                        vector.tensor_tensor(diff[:],
                                             ln1_fm[:, :, LP:SC + LP],
                                             ln1_fm[:, :, LP - 1:SC + LP - 1],
                                             ALU.subtract).then_inc(s_diff, 1)
                        wseq(vector, s_kvrmm, 24 * (s + 1))
                        for pi, xm in enumerate((xmk, xmv, diff)):
                            for ic in range(NCC):
                                vector.tensor_scalar(
                                    xm[:, ic, :], diff[:, ic, :],
                                    ((mkc, mvc, mrc)[pi])[:, ic:ic + 1],
                                    None, ALU.mult)
                            vector.tensor_tensor(
                                xm[:], xm[:],
                                ln1_fm[:, :, LP - 1:SC + LP - 1], ALU.add
                            ).then_inc(s_mix, 1)
                    for tb in range(TB):
                        tau = s * TB + tb
                        q = tau
                        vector.wait_ge(s_ostat, tau + 1)
                        vector.tensor_tensor(st2[:, 4, tb:tb + 1],
                                             st2[:, 4, tb:tb + 1],
                                             st2[:, 5, tb:tb + 1], ALU.add)
                        vector.tensor_tensor(st2[:, 5, tb:tb + 1],
                                             st2[:, 2, tb:tb + 1],
                                             st2[:, 3, tb:tb + 1], ALU.add)
                        vector.tensor_scalar(st2[:, 0, tb:tb + 1],
                                             st2[:, 4, tb:tb + 1],
                                             1.0 / C, None, ALU.mult)
                        vector.tensor_scalar(st2[:, 2, tb:tb + 1],
                                             st2[:, 5, tb:tb + 1],
                                             1.0 / C, EPS, ALU.mult, ALU.add)
                        vsync(vector, vector.tensor_tensor(
                            st2[:, 1, tb:tb + 1], st2[:, 0, tb:tb + 1],
                            st2[:, 0, tb:tb + 1], ALU.mult))
                        vector.tensor_tensor(st2[:, 2, tb:tb + 1],
                                             st2[:, 2, tb:tb + 1],
                                             st2[:, 1, tb:tb + 1],
                                             ALU.subtract).then_inc(s_ch2, 1)
                        vector.wait_ge(s_le2, tau + 1)
                        vector.scalar_tensor_tensor(st2[:, 1, tb:tb + 1],
                                                    st2[:, 0, tb:tb + 1],
                                                    -1.0,
                                                    st2[:, 3, tb:tb + 1],
                                                    ALU.mult, ALU.mult
                                                    ).then_inc(s_nb2, 1)
                        vector.wait_ge(s_nb2, tau + 1)
                        vector.tensor_scalar(xres[:, tb, :],
                                             xres[:, tb, :],
                                             st2[:, 1, tb:tb + 1], None,
                                             ALU.add)
                        vector.scalar_tensor_tensor(
                            xp_v[:, tb, 0:512], pb_olo[tb % 2][:],
                            st2[:, 3, tb:tb + 1], xres[:, tb, 0:512],
                            ALU.mult, ALU.add)
                        vector.scalar_tensor_tensor(
                            xp_v[:, tb, 512:1024], pb_ohi[tb % 2][:],
                            st2[:, 3, tb:tb + 1], xres[:, tb, 512:1024],
                            ALU.mult, ALU.add).then_inc(s_xp, 1)
                        vector.reduce_sum(st3[:, 4, tb:tb + 1],
                                          xp_v[:, tb, :],
                                          mybir.AxisListType.X)
                    vector.wait_ge(s_sq3, TB * (s + 1))
                    vector.tensor_tensor(st3[:, 5, :], st3[:, 2, :],
                                         st3[:, 3, :], ALU.add)
                    vector.tensor_scalar(st3[:, 0, :], st3[:, 4, :],
                                         1.0 / C, None, ALU.mult)
                    vector.tensor_scalar(st3[:, 2, :], st3[:, 5, :],
                                         1.0 / C, EPS, ALU.mult, ALU.add)
                    vsync(vector, vector.tensor_tensor(
                        st3[:, 1, :], st3[:, 0, :], st3[:, 0, :], ALU.mult))
                    vector.tensor_tensor(st3[:, 2, :], st3[:, 2, :],
                                         st3[:, 1, :], ALU.subtract
                                         ).then_inc(s_ch3, 1)
                    vector.wait_ge(s_le3, s + 1)
                    vector.scalar_tensor_tensor(st3[:, 1, :], st3[:, 0, :],
                                                -1.0, st3[:, 3, :],
                                                ALU.mult, ALU.mult
                                                ).then_inc(s_nb3, 1)
                    if s + 1 < nsc:
                        for tb in range(TB):
                            q = (s + 1) * TB + tb
                            vector.wait_ge(s_slot[q % XR], xq_ready[q])
                            vector.tensor_copy(xres[:, tb, :],
                                               x_tm[:, q % XR, :]
                                               ).then_inc(s_xcp, 1)
                    for tb in range(TB):
                        tau = s * TB + tb
                        vector.wait_ge(s_fstat, tau + 1)
                        vector.tensor_tensor(st4[:, 4, tb:tb + 1],
                                             st4[:, 4, tb:tb + 1],
                                             st4[:, 5, tb:tb + 1], ALU.add)
                        vector.tensor_tensor(st4[:, 5, tb:tb + 1],
                                             st4[:, 2, tb:tb + 1],
                                             st4[:, 3, tb:tb + 1], ALU.add)
                        vector.tensor_scalar(st4[:, 0, tb:tb + 1],
                                             st4[:, 4, tb:tb + 1],
                                             1.0 / C, None, ALU.mult)
                        vector.tensor_scalar(st4[:, 2, tb:tb + 1],
                                             st4[:, 5, tb:tb + 1],
                                             1.0 / C, EPS, ALU.mult, ALU.add)
                        vsync(vector, vector.tensor_tensor(
                            st4[:, 1, tb:tb + 1], st4[:, 0, tb:tb + 1],
                            st4[:, 0, tb:tb + 1], ALU.mult))
                        vector.tensor_tensor(st4[:, 2, tb:tb + 1],
                                             st4[:, 2, tb:tb + 1],
                                             st4[:, 1, tb:tb + 1],
                                             ALU.subtract).then_inc(s_ch4, 1)
                        vector.wait_ge(s_le4, tau + 1)
                        vector.scalar_tensor_tensor(st4[:, 1, tb:tb + 1],
                                                    st4[:, 0, tb:tb + 1],
                                                    -1.0,
                                                    st4[:, 3, tb:tb + 1],
                                                    ALU.mult, ALU.mult
                                                    ).then_inc(s_nb4, 1)
                        vector.wait_ge(s_nb4, tau + 1)
                        vector.wait_ge(s_l3ap, tau + 1)
                        vector.tensor_scalar(xp_v[:, tb, :], xp_v[:, tb, :],
                                             st4[:, 1, tb:tb + 1], None,
                                             ALU.add)
                        wseq(vector, s_od, 16 * tau)
                        vector.scalar_tensor_tensor(
                            out_r[:, 0:512], pb_olo[tb % 2][:],
                            st4[:, 3, tb:tb + 1], xp_v[:, tb, 0:512],
                            ALU.mult, ALU.add)
                        vector.scalar_tensor_tensor(
                            out_r[:, 512:1024], pb_ohi[tb % 2][:],
                            st4[:, 3, tb:tb + 1], xp_v[:, tb, 512:1024],
                            ALU.mult, ALU.add).then_inc(s_out, 1)
    return nc


def kernel(L0_attn_x, L0_attn_mix_k, L0_attn_mix_v, L0_attn_mix_r,
           L0_attn_Wk, L0_attn_Wv, L0_attn_Wr, L0_attn_Wo,
           L0_attn_time_decay, L0_attn_time_first, L0_ffn_weight):
    B, T, _ = L0_attn_x.shape
    nsc = T // SC
    x = np.ascontiguousarray(np.asarray(L0_attn_x, dtype=np.float32))

    def wT(w):
        return np.ascontiguousarray(np.asarray(w, dtype=np.float32).T)

    def cols(v):
        return np.ascontiguousarray(
            np.asarray(v, dtype=np.float32).reshape(NCC, 128).T)

    tf = np.asarray(L0_attn_time_first, dtype=np.float32)
    td = np.asarray(L0_attn_time_decay, dtype=np.float32)
    u_cols = cols(np.repeat(tf, C // H))

    p_ge64 = (np.arange(128) >= 64).astype(np.int64)
    parity = ((np.arange(H)[:, None] % 2) == p_ge64[None, :]
              ).astype(np.float32)
    hm8 = (np.arange(H)[:, None] // 2 == np.arange(NCC)[None, :]
           ).astype(np.float32)
    head_of = 2 * np.arange(NCC)[:, None, None] + p_ge64[None, :, None]
    hm16 = (np.arange(H)[None, None, :] == head_of).astype(bf16)
    sel2 = np.ascontiguousarray(np.transpose(hm16, (0, 2, 1)))

    shared = {
        "wkT": wT(L0_attn_Wk), "wvT": wT(L0_attn_Wv), "wrT": wT(L0_attn_Wr),
        "woT": wT(L0_attn_Wo), "wfT": wT(L0_ffn_weight),
        "mkc": cols(np.asarray(L0_attn_mix_k).reshape(-1)),
        "mvc": cols(np.asarray(L0_attn_mix_v).reshape(-1)),
        "mrc": cols(np.asarray(L0_attn_mix_r).reshape(-1)),
        "ucols": u_cols,
        "tdcol": np.ascontiguousarray(td.reshape(H, 1)),
        "tfcol": np.ascontiguousarray(tf.reshape(H, 1)),
        "parity": parity, "hm8": hm8,
        "hm16": np.ascontiguousarray(hm16),
        "sel2": sel2,
    }
    nc = build_program(nsc)
    in_maps = [dict(shared, x=np.ascontiguousarray(x[b])) for b in range(B)]
    res = run_bass_kernel_spmd(nc, in_maps, core_ids=list(range(B)))
    return np.stack([np.asarray(r["out"], dtype=np.float32)
                     for r in res.results], axis=0)
